# version 09
# baseline (speedup 1.0000x reference)
import contextlib
import numpy as np
import ml_dtypes
import concourse.bass as bass
import concourse.mybir as mybir
from concourse.bass_utils import run_bass_kernel_spmd

F32 = mybir.dt.float32
BF16 = mybir.dt.bfloat16
AF = mybir.ActivationFunctionType
ALU = mybir.AluOpType
AX = mybir.AxisListType
BFNP = ml_dtypes.bfloat16

D = 2048
T = 4096
NT = 1024
NDC = 16
NBLK = 16
BT = 256
KB = 1024
DEBUG = False



class Plan:

    ENGINES = ("pe", "act", "dve", "sp", "gp")

    NLANES = 8

    def __init__(self, nc):
        self.nc = nc
        self.sems = {e: nc.alloc_semaphore("sem_" + e) for e in ("pe", "act", "dve")}
        self.recs = {e: [] for e in self.ENGINES}
        self.last_writer = {}
        self.readers = {}
        self.lane = {q: [nc.alloc_semaphore(f"ln_{q}{i}") for i in range(self.NLANES)]
                     for q in ("sp", "gp")}
        self.lane_n = {q: [0] * self.NLANES for q in ("sp", "gp")}
        self.lane_last = {q: [None] * self.NLANES for q in ("sp", "gp")}
        self.lane_i = {q: 0 for q in ("sp", "gp")}

    @staticmethod
    def _name(ap):
        t = getattr(ap, "tensor", None)
        n = getattr(t, "name", None)
        return n if n is not None else str(id(ap))

    def op(self, eng, fn, ins=(), outs=(), tag="", deps=(), dma=False):
        rec = {"eng": eng, "fn": fn, "inc": False, "val": None,
               "waits": [], "tag": tag, "semh": None}
        if dma:
            L = self.lane_i[eng] % self.NLANES
            self.lane_i[eng] += 1
            prev = self.lane_last[eng][L]
            if prev is not None:
                self._need(rec, prev, force=True)
            self.lane_n[eng][L] += 1
            rec["inc"] = True
            rec["semh"] = self.lane[eng][L]
            rec["val"] = 16 * self.lane_n[eng][L]
            self.lane_last[eng][L] = rec
        for w in deps:
            self._need(rec, w, force=True)
        for a in ins:
            self._need(rec, self.last_writer.get(self._name(a)))
        for a in outs:
            n = self._name(a)
            self._need(rec, self.last_writer.get(n))
            seen = set()
            for r in reversed(self.readers.get(n, ())):
                if r["eng"] not in seen:
                    seen.add(r["eng"])
                    self._need(rec, r)
        for a in outs:
            n = self._name(a)
            self.last_writer[n] = rec
            self.readers[n] = []
        for a in ins:
            self.readers.setdefault(self._name(a), []).append(rec)
        self.recs[eng].append(rec)
        return rec

    def _need(self, rec, w, force=False):
        if w is None or w is rec:
            return
        if force and w["eng"] == rec["eng"] and rec["eng"] in ("pe", "act", "dve"):
            w["inc"] = True
            for (x, _) in rec["waits"]:
                if x is w:
                    return
            rec["waits"].append((w, None))
            return
        if not force and w["eng"] == rec["eng"] and rec["eng"] in ("pe", "sp", "gp"):
            return
        w["inc"] = True
        for (x, _) in rec["waits"]:
            if x is w:
                return
        rec["waits"].append((w, None))

    def barrier(self):
        if getattr(self, "flush_fns", None):
            if not hasattr(self, "flush_prev"):
                self.flush_prev = {}
            for eng, fn, out in self.flush_fns:
                dep = self.flush_prev.get(eng)
                r = self.op(eng, fn, ins=(), outs=(out,), tag="flush",
                            deps=(dep,) if dep else ())
                self.flush_prev[eng] = r
        lasts = {}
        for e in ("pe", "act", "dve"):
            lasts[e] = [self.recs[e][-1]] if self.recs[e] else []
        for q in ("sp", "gp"):
            lasts[q] = [r for r in self.lane_last[q] if r is not None]
        for e in self.ENGINES:
            rec = {"eng": e, "fn": (lambda eng: eng.nop()), "inc": False,
                   "val": None, "waits": [], "tag": "barrier", "semh": None}
            for e2 in self.ENGINES:
                for w in lasts[e2]:
                    if e2 != e or e2 in ("sp", "gp"):
                        if w is not rec:
                            w["inc"] = True
                            rec["waits"].append((w, None))
            self.recs[e].append(rec)
        self.last_writer = {}
        self.readers = {}

    def final_dma_fence(self):
        for q in ("sp", "gp"):
            rec = {"eng": q, "fn": (lambda eng: eng.nop()), "inc": False,
                   "val": None, "waits": [], "tag": "fence", "semh": None}
            for w in self.lane_last[q]:
                if w is not None:
                    w["inc"] = True
                    rec["waits"].append((w, None))
            self.recs[q].append(rec)

    def finalize(self):
        for e in ("pe", "act", "dve"):
            tot = 0
            for rec in self.recs[e]:
                if rec["inc"]:
                    tot += 1
                    rec["val"] = tot
                    rec["semh"] = self.sems[e]
        for e in self.ENGINES:
            for rec in self.recs[e]:
                rec["rw"] = [(w["semh"], w["val"]) for (w, _) in rec["waits"]]

    def run_stream(self, ename, eng):
        waited = {}
        for rec in self.recs[ename]:
            for (sem, val) in sorted(rec["rw"], key=lambda t: t[1]):
                if waited.get(sem.name, -1) < val:
                    eng.wait_ge(sem, val)
                    waited[sem.name] = val
            inst = rec["fn"](eng)
            if rec["inc"]:
                inst.then_inc(rec["semh"], 16 if ename in ("sp", "gp") else 1)




def build_program():
    nc = bass.Bass("TRN2", target_bir_lowering=False, debug=False)

    DI = nc.dram_tensor
    d_xbt = DI("xbt", [D, T], F32, kind="ExternalInput").ap()
    d_xot = DI("xot", [D, NT], F32, kind="ExternalInput").ap()
    d_wq = DI("wq_l", [16, 128, 16, 128], F32, kind="ExternalInput").ap()
    d_wk = DI("wk_l", [128, 16, 512], F32, kind="ExternalInput").ap()
    d_wv = DI("wv_l", [128, 16, 512], F32, kind="ExternalInput").ap()
    d_wo = DI("wo_l", [16, 128, 16, 128], F32, kind="ExternalInput").ap()
    d_rw = DI("rw_l", [128, 16, 8], F32, kind="ExternalInput").ap()
    d_wg = DI("wg_l", [8, 16, 128, 16, 128], F32, kind="ExternalInput").ap()
    d_wu = DI("wu_l", [8, 16, 128, 16, 128], F32, kind="ExternalInput").ap()
    d_wd = DI("wd_l", [8, 16, 128, 16, 128], F32, kind="ExternalInput").ap()
    d_wsg = DI("wsg_l", [16, 128, 16, 128], F32, kind="ExternalInput").ap()
    d_wsu = DI("wsu_l", [16, 128, 16, 128], F32, kind="ExternalInput").ap()
    d_wsd = DI("wsd_l", [16, 128, 16, 128], F32, kind="ExternalInput").ap()
    d_id = DI("identf", [128, 128], F32, kind="ExternalInput").ap()
    d_rt = DI("rotT", [128, 128], BF16, kind="ExternalInput").ap()
    d_ck = DI("crepk", [128, T], BF16, kind="ExternalInput").ap()
    d_sk = DI("srepk", [128, T], BF16, kind="ExternalInput").ap()
    d_cq = DI("crepq", [128, NT], BF16, kind="ExternalInput").ap()
    d_sq = DI("srepq", [128, NT], BF16, kind="ExternalInput").ap()
    d_msk = DI("cmask", [128, 16 * 512], BF16, kind="ExternalInput").ap()
    d_cn = DI("consts", [128, 8], F32, kind="ExternalInput").ap()
    d_y = DI("y", [NT, D], F32, kind="ExternalOutput").ap()
    if DEBUG:
        d_dxl = DI("dbg_xlnown", [128, NDC * NT], BF16, kind="ExternalOutput").ap()
        d_dk = DI("dbg_k", [128, 4 * T], BF16, kind="ExternalOutput").ap()
        d_dv = DI("dbg_v", [128, 32 * 520], BF16, kind="ExternalOutput").ap()
        d_dq = DI("dbg_q", [128, 2 * NT], BF16, kind="ExternalOutput").ap()
        d_dp = DI("dbg_p", [128, 1024], BF16, kind="ExternalOutput").ap()
        d_dctx = DI("dbg_ctx", [128, NDC * NT], BF16, kind="ExternalOutput").ap()
        d_dx1 = DI("dbg_x1", [128, NDC * NT], F32, kind="ExternalOutput").ap()
        d_dx0 = DI("dbg_x0", [128, NDC * NT], F32, kind="ExternalOutput").ap()
        d_dlg = DI("dbg_lg", [8, NT], F32, kind="ExternalOutput").ap()
        d_dg = DI("dbg_g", [128, 72], F32, kind="ExternalOutput").ap()
        d_dxs = DI("dbg_xs", [128, NDC * NT], BF16, kind="ExternalOutput").ap()

    BASE = (nc.sbuf_base + 31) // 32 * 32
    R0 = BASE
    R1 = R0 + 32 * KB
    R2 = R1 + 33 * KB
    R3 = R2 + 32 * KB
    R5 = R3 + 32 * KB
    RES = R5 + 13 * KB
    END = RES + 64 * KB
    cap = nc.sbuf_base + nc.sbuf_bytes_remaining
    assert END <= cap, (END, cap)

    def at(name, shape, dtype, off):
        return nc.alloc_sbuf_tensor_at(name, shape, dtype, offset=off).ap()

    k_sb = at("k_sb", [128, 4 * T], BF16, R0)
    v_sb = at("v_sb", [128, 32 * 520], BF16, R1)
    xln_own = at("xln_own", [128, NDC * NT], BF16, R2)
    ctx_sb = at("ctx_sb", [128, NDC * NT], BF16, R3)
    resid = at("resid", [128, NDC * NT], F32, RES)

    _r5 = [R5]

    def r5(name, shape, dtype):
        nbytes = shape[1] * (4 if dtype == F32 else (2 if dtype == BF16 else 4))
        off = (_r5[0] + 31) // 32 * 32
        _r5[0] = off + nbytes
        assert _r5[0] <= R5 + 13 * KB, (name, _r5[0] - R5)
        return at(name, shape, dtype, off)

    consts = r5("consts_sb", [128, 8], F32)
    ones_bf = r5("ones_bf", [128, 1], BF16)
    ones_f = r5("ones_f", [128, 128], F32)
    ident = r5("ident_sb", [128, 128], F32)
    rt_sb = r5("rt_sb", [128, 128], BF16)
    gstore = r5("gstore", [128, 9 * 8], F32)
    gtmp = r5("gtmp", [16, 128], F32)
    lrow = r5("lrow", [128, 512], F32)
    cfj = r5("cfj", [128, 8], BF16)
    jnk_a = r5("jnk_a", [128, 8], F32)
    jnk_b = r5("jnk_b", [128, 8], F32)
    jnk_c = r5("jnk_c", [128, 8], F32)

    rows5 = [at(f"rows5_{i}", [1, 4 * 256], F32, RES + 48 * KB + 4 * KB * i)
             for i in range(2)]
    a_bc = [at(f"a_bc{i}", [128, 256], F32, RES + 56 * KB + KB * i) for i in range(2)]
    b_bc = [at(f"b_bc{i}", [128, 256], F32, RES + 58 * KB + KB * i) for i in range(2)]
    ka_t = at("ka_t", [128, 256], BF16, RES + 60 * KB)
    kt3 = at("kt3", [128, 256], F32, RES + 60 * KB + 512)
    kt4 = at("kt4", [128, 256], F32, RES + 60 * KB + 1536)

    wk_sb = at("wk_sb", [128, NDC * 512], BF16, RES)
    wv_sb = at("wv_sb", [128, NDC * 512], BF16, RES + 16 * KB)
    crepk = at("crepk_sb", [128, T], BF16, RES + 32 * KB)
    srepk = at("srepk_sb", [128, T], BF16, RES + 40 * KB)
    xblk = [at(f"xblk{i}", [128, NDC * BT], BF16, R3 + 8 * KB * i) for i in range(2)]
    xsq = [at(f"xsq{i}", [128, NDC * BT], BF16, R3 + 16 * KB + 8 * KB * i) for i in range(2)]

    crepq = at("crepq_sb", [128, NT], BF16, RES)
    srepq = at("srepq_sb", [128, NT], BF16, RES + 2 * KB)
    msk_sb = at("cmask_sb", [128, 16 * 512], BF16, RES + 4 * KB)
    pslot = [at(f"pslot{i}", [128, 1024], BF16, RES + 20 * KB + 2 * KB * i)
             for i in range(4)]
    qslot = [at(f"qslot{i}", [128, NT], BF16, RES + 28 * KB + 2 * KB * i)
             for i in range(2)]
    qa = at("qa", [128, NT], BF16, RES + 32 * KB)
    t3h = at("t3h", [128, 512], F32, RES + 34 * KB)
    t4h = at("t4h", [128, 512], F32, RES + 36 * KB)
    wqslab = [at(f"wqslab{i}", [128, NDC * 128], BF16, RES + 38 * KB + 4 * KB * i)
              for i in range(2)]
    linv_bc = at("linv_bc", [128, 512], F32, RES + 46 * KB)
    woslab = [at(f"woslab{i}", [128, NDC * 128], BF16, R1 + 4 * KB * i)
              for i in range(2)]
    abo_bc = at("abo_bc", [128, 1024], F32, RES + 48 * KB)
    sqo = at("sqo", [128, 512], BF16, RES + 52 * KB)
    ro_m = at("ro_m", [1, 512], F32, RES + 53 * KB)
    ro_q = at("ro_q", [1, 512], F32, RES + 55 * KB)
    ro_a = at("ro_a", [1, 512], F32, RES + 57 * KB)
    ro_b = at("ro_b", [1, 512], F32, RES + 59 * KB)

    ltile = at("ltile", [128, 8], F32, R1)
    sq1 = at("sq1", [128, 512], BF16, R1 + 64)
    r2a = at("r2a", [1, 512], F32, R1 + 1 * KB + 64)
    r2b = at("r2b", [1, 512], F32, R1 + 3 * KB + 64)
    r2m = at("r2m", [1, 512], F32, R1 + 5 * KB + 64)
    r2q = at("r2q", [1, 512], F32, R1 + 7 * KB + 64)
    ab2_bc = at("ab2_bc", [128, 1024], F32, R1 + 9 * KB + 64)
    lg_sb = at("lg_sb", [8, NT], F32, R1 + 13 * KB + 64)
    gm = at("gm", [128, 1], F32, R1 + 17 * KB + 64)
    geq = at("geq", [128, 8], F32, R1 + 17 * KB + 128)
    grun = at("grun", [128, 1], F32, R1 + 17 * KB + 192)
    gnr = at("gnr", [128, 1], F32, R1 + 17 * KB + 256)
    rowtmp = at("rowtmp", [1, NT], BF16, R1 + 18 * KB)
    onesrow_bf = at("onesrow_bf", [1, 128], BF16, R1 + 20 * KB)
    s_bc = at("s_bc", [128, 512], F32, R1 + 20 * KB + 512)
    gslab = [at(f"gslab{i}", [128, NDC * 128], BF16, R1 + 4 * KB * i) for i in range(2)]
    uslab = [at(f"uslab{i}", [128, NDC * 128], BF16, R1 + 8 * KB + 4 * KB * i)
             for i in range(2)]
    dslab = [at(f"dslab{i}", [128, NDC * 128], BF16, R1 + 16 * KB + 4 * KB * i)
             for i in range(2)]
    mskbc = at("mskbc", [128, NT], BF16, R1 + 24 * KB)
    s1s = [at(f"s1s{i}", [128, 512], BF16, R1 + 26 * KB + KB * i) for i in range(2)]
    ums = [at(f"ums{i}", [128, 512], BF16, R1 + 28 * KB + KB * i) for i in range(2)]
    rowtmp5 = at("rowtmp5", [1, NT], BF16, R1 + 30 * KB)
    onesrow5 = at("onesrow5", [1, 128], BF16, R1 + 32 * KB)
    gtmp5 = at("gtmp5", [16, 128], F32, R1 + 32 * KB + 256)

    h_sb = at("h_sb", [128, NDC * NT], BF16, R0)
    ostg = [at(f"ostg{i}", [128, D], F32, R0 + 8 * KB * i) for i in range(2)]

    p = Plan(nc)

    def dma(q, out, in_, tag="", slow=False):
        def f(e, out=out, in_=in_, slow=slow):
            if slow:
                return e.dma_start(out=out, in_=in_, allow_slow_non_contiguous=True)
            return e.dma_start(out=out, in_=in_)
        return p.op(q, f, ins=[in_], outs=[out], tag=tag, dma=True)

    dmac = dma

    def mm(out, lhsT, rhs, start, stop, tag=""):
        def f(e, out=out, lhsT=lhsT, rhs=rhs, start=start, stop=stop):
            return e.matmul(out, lhsT, rhs, start=start, stop=stop,
                            skip_group_check=True)
        return p.op("pe", f, ins=[lhsT, rhs], outs=[out], tag=tag)

    def tr(out, in_, tag=""):
        np_ = in_.shape[0]
        idv = ident[0:np_, 0:np_]

        def f(e, out=out, in_=in_, idv=idv):
            return e.transpose(out, in_, idv)
        return p.op("pe", f, ins=[in_, ident], outs=[out], tag=tag)

    def act(out, in_, func, bias=None, scale=None, tag=""):
        ins = [in_] + [x for x in (bias, scale)
                       if x is not None and not isinstance(x, float)]

        def f(e, out=out, in_=in_, func=func, bias=bias, scale=scale):
            kw = {}
            if bias is not None:
                kw["bias"] = bias
            if scale is not None:
                kw["scale"] = scale
            return e.activation(out, in_, func, **kw)
        return p.op("act", f, ins=ins, outs=[out], tag=tag)

    def acopy(out, in_, tag=""):
        def f(e, out=out, in_=in_):
            return e.copy(out, in_)
        return p.op("act", f, ins=[in_], outs=[out], tag=tag)

    def tt(out, in0, in1, op, tag="", deps=()):
        def f(e, out=out, in0=in0, in1=in1, op=op):
            return e.tensor_tensor(out, in0, in1, op)
        return p.op("dve", f, ins=[in0, in1], outs=[out], tag=tag, deps=deps)

    def ts(out, in0, s1, op, s2=None, op2=None, tag=""):
        ins = [in0] + [x for x in (s1, s2)
                       if x is not None and not isinstance(x, float)]

        def f(e, out=out, in0=in0, s1=s1, s2=s2, op=op, op2=op2):
            if op2 is None:
                return e.tensor_scalar(out, in0, s1, None, op)
            return e.tensor_scalar(out, in0, s1, s2, op, op2)
        return p.op("dve", f, ins=ins, outs=[out], tag=tag)

    def stt(out, in0, s, in1, op0, op1, tag=""):
        ins = [in0, in1] + ([s] if not isinstance(s, float) else [])

        def f(e, out=out, in0=in0, s=s, in1=in1, op0=op0, op1=op1):
            return e.scalar_tensor_tensor(out, in0, s, in1, op0, op1)
        return p.op("dve", f, ins=ins, outs=[out], tag=tag)

    def vcopy(out, in_, tag=""):
        def f(e, out=out, in_=in_):
            return e.tensor_copy(out, in_)
        return p.op("dve", f, ins=[in_], outs=[out], tag=tag)

    def vmemset(out, val, tag=""):
        def f(e, out=out, val=val):
            return e.memset(out, val)
        return p.op("dve", f, ins=[], outs=[out], tag=tag)

    def vrecip(out, in_, tag=""):
        def f(e, out=out, in_=in_):
            return e.reciprocal(out, in_)
        return p.op("dve", f, ins=[in_], outs=[out], tag=tag)

    def vredmax(out, in_, tag=""):
        def f(e, out=out, in_=in_):
            return e.reduce_max(out, in_, axis=AX.X)
        return p.op("dve", f, ins=[in_], outs=[out], tag=tag)

    def _fl_dve(e):
        return e.tensor_copy(jnk_b[:], jnk_a[:])

    def _fl_act(e):
        return e.copy(jnk_c[:], jnk_a[:])

    p.flush_fns = [("dve", _fl_dve, jnk_b), ("act", _fl_act, jnk_c)]

    CN_EXPB = consts[:, 0:1]
    R_INV = consts[0:1, 1:2]
    R_EPS = consts[0:1, 2:3]
    CN_ONE = consts[:, 3:4]
    CN_NEG1 = consts[:, 5:6]
    R_NEG1 = consts[0:1, 5:6]

    dma("sp", consts, d_cn)
    dma("sp", ident, d_id)
    dma("sp", rt_sb, d_rt)
    dma("sp", crepk, d_ck)
    dma("sp", srepk, d_sk)
    vmemset(jnk_a, 0.0)
    vmemset(cfj, 0.0)
    vmemset(ones_bf, 1.0)
    vmemset(ones_f, 1.0)
    vmemset(v_sb, 1.0)
    dmac("gp", wk_sb.rearrange("p (c m) -> p c m", c=NDC), d_wk, tag="wk")
    dmac("gp", wv_sb.rearrange("p (c m) -> p c m", c=NDC), d_wv, tag="wv")

    st1 = contextlib.ExitStack()
    ps_k = [st1.enter_context(nc.psum_tensor(f"ps_k{i}", [128, 256], F32)).ap()
            for i in range(2)]
    ps_v = [st1.enter_context(nc.psum_tensor(f"ps_v{i}", [128, 512], F32)).ap()
            for i in range(2)]
    ps_st = [st1.enter_context(nc.psum_tensor(f"ps_st{i}", [1, 512], F32)).ap()
             for i in range(2)]
    ps_bb = [st1.enter_context(nc.psum_tensor(f"ps_bb{i}", [128, 256], F32)).ap()
             for i in range(2)]

    def p1_load(bi):
        xb = xblk[bi % 2]
        t0 = bi * BT
        dmac("gp", xb.rearrange("p (c t) -> p c t", c=NDC),
             d_xbt[:, t0:t0 + BT].rearrange("(c p) t -> p c t", p=128),
             tag=f"xb{bi}")

    def p1_ln(bi):
        xb = xblk[bi % 2]
        xq = xsq[bi % 2]
        ab = a_bc[bi % 2]
        bb = b_bc[bi % 2]
        stp = ps_st[bi % 2]
        tt(xq, xb, xb, ALU.mult, tag=f"xsq{bi}")
        for c in range(NDC):
            mm(stp[0:1, 0:256], ones_bf, xb[:, c * BT:(c + 1) * BT],
               start=(c == 0), stop=(c == NDC - 1), tag=f"sx{bi}")
        for c in range(NDC):
            mm(stp[0:1, 256:512], ones_bf, xq[:, c * BT:(c + 1) * BT],
               start=False, stop=(c == NDC - 1), tag=f"sq{bi}")
        rw = rows5[bi % 2]
        r_mean, r_var, r_a, r_b = (rw[0:1, i * 256:(i + 1) * 256]
                                   for i in range(4))
        ts(r_mean, stp[0:1, 0:256], R_INV, ALU.mult, tag="mean")
        ts(r_var, stp[0:1, 256:512], R_INV, ALU.mult, tag="msq")
        tt(r_a, r_mean, r_mean, ALU.mult, tag="m2")
        tt(r_var, r_var, r_a, ALU.subtract, tag="var")
        act(r_a, r_var, AF.Sqrt, bias=R_EPS, tag="std")
        vrecip(r_a, r_a, tag="rstd")
        stt(r_b, r_mean, R_NEG1, r_a, ALU.mult, ALU.mult, tag="brow")
        mm(ps_bb[0], ones_f[0:1, :], r_a, start=True, stop=True, tag="bca")
        mm(ps_bb[1], ones_f[0:1, :], r_b, start=True, stop=True, tag="bcb")
        acopy(ab, ps_bb[0], tag="abc")
        acopy(bb, ps_bb[1], tag="bbc")

    def p1_apply(bi):
        xb = xblk[bi % 2]
        xq = xsq[bi % 2]
        ab = a_bc[bi % 2]
        bb = b_bc[bi % 2]
        for c in range(NDC):
            tt(xq[:, c * BT:(c + 1) * BT], xb[:, c * BT:(c + 1) * BT], ab,
               ALU.mult, tag=f"lm{bi}_{c}")
        for c in range(NDC):
            tt(xq[:, c * BT:(c + 1) * BT], xq[:, c * BT:(c + 1) * BT], bb,
               ALU.add, tag=f"la{bi}_{c}")

    def p1_kv(bi):
        t0 = bi * BT
        xln = xsq[bi % 2]
        for kc in range(4):
            pk = ps_k[kc % 2]
            for c in range(NDC):
                mm(pk, wk_sb[:, c * 512 + kc * 128: c * 512 + (kc + 1) * 128],
                   xln[:, c * BT:(c + 1) * BT],
                   start=(c == 0), stop=(c == NDC - 1), tag=f"kp{bi}_{kc}")
            acopy(ka_t, pk, tag=f"ka{bi}_{kc}")
            mm(pk, rt_sb, ka_t, start=True, stop=True, tag=f"kr{bi}_{kc}")
            tt(kt3, ka_t, crepk[:, t0:t0 + BT], ALU.mult, tag=f"k3{bi}_{kc}")
            tt(kt4, pk, srepk[:, t0:t0 + BT], ALU.mult, tag=f"k4{bi}_{kc}")
            tt(k_sb[:, kc * T + t0: kc * T + t0 + BT], kt3, kt4, ALU.add,
               tag=f"ks{bi}_{kc}")
        xln3 = xln.rearrange("p (c t) -> p c t", c=NDC)
        for sub in range(2):
            pv = ps_v[sub]
            for c in range(NDC):
                mm(pv, xln3[:, c, sub * 128:(sub + 1) * 128],
                   wv_sb[:, c * 512:(c + 1) * 512],
                   start=(c == 0), stop=(c == NDC - 1), tag=f"vp{bi}_{sub}")
            kt = bi * 2 + sub
            vdst = v_sb[:, kt * 520:(kt + 1) * 520].rearrange(
                "p (h e) -> p h e", h=8)[:, :, 0:64]
            acopy(vdst, pv.rearrange("p (h e) -> p h e", h=8), tag=f"ve{bi}_{sub}")

    p1_load(0)
    p1_load(1)
    p1_ln(0)
    p1_apply(0)
    for bi in range(NBLK):
        if bi + 2 < NBLK:
            p1_load(bi + 2)
        if bi + 1 < NBLK:
            p1_ln(bi + 1)
        p1_kv(bi)
        if bi + 1 < NBLK:
            p1_apply(bi + 1)

    st1.close()
    p.barrier()

    dmac("gp", xln_own.rearrange("p (c t) -> p c t", c=NDC),
         d_xot.rearrange("(c p) t -> p c t", p=128), tag="xob")
    dma("sp", crepq, d_cq)
    dma("sp", srepq, d_sq)
    dma("sp", msk_sb, d_msk)

    st2 = contextlib.ExitStack()
    ps_sc = [st2.enter_context(nc.psum_tensor(f"ps_sc{i}", [128, 1024], F32)).ap()
             for i in range(3)]
    ps_ctx = [st2.enter_context(nc.psum_tensor(f"ps_ctx{i}", [128, 512], F32)).ap()
              for i in range(2)]
    ps_q = ps_sc[1]

    for th in range(2):
        for c in range(NDC):
            xc = xln_own[:, c * NT + th * 512: c * NT + (th + 1) * 512]
            tt(sqo, xc, xc, ALU.mult, tag=f"oq{th}_{c}")
            mm(ps_q[0:1, 0:512], ones_bf, xc,
               start=(c == 0), stop=(c == NDC - 1), tag=f"osx{th}")
            mm(ps_q[0:1, 512:1024], ones_bf, sqo,
               start=(c == 0), stop=(c == NDC - 1), tag=f"osq{th}")
        ts(ro_m, ps_q[0:1, 0:512], R_INV, ALU.mult, tag="omean")
        ts(ro_q, ps_q[0:1, 512:1024], R_INV, ALU.mult, tag="omsq")
        tt(ro_a, ro_m, ro_m, ALU.mult, tag="om2")
        tt(ro_a, ro_q, ro_a, ALU.subtract, tag="ovar")
        act(ro_a, ro_a, AF.Sqrt, bias=R_EPS, tag="ostd")
        vrecip(ro_a, ro_a, tag="orstd")
        stt(ro_b, ro_m, R_NEG1, ro_a, ALU.mult, ALU.mult, tag="obrow")
        mm(ps_q[:, 0:512], ones_f[0:1, :], ro_a, start=True, stop=True, tag="oba")
        mm(ps_q[:, 512:1024], ones_f[0:1, :], ro_b, start=True, stop=True, tag="obb")
        acopy(abo_bc, ps_q, tag="oab")
        for c in range(NDC):
            xc = xln_own[:, c * NT + th * 512: c * NT + (th + 1) * 512]
            tt(xc, xc, abo_bc[:, 0:512], ALU.mult, tag=f"olm{th}_{c}")
        for c in range(NDC):
            xc = xln_own[:, c * NT + th * 512: c * NT + (th + 1) * 512]
            tt(xc, xc, abo_bc[:, 512:1024], ALU.add, tag=f"ola{th}_{c}")

    for pair in range(16):
        ws = wqslab[pair % 2]
        dmac("gp", ws.rearrange("p (c m) -> p c m", c=NDC), d_wq[pair],
             tag=f"wq{pair}")
        for th in range(2):
            for c in range(NDC):
                mm(ps_q[:, th * 512:(th + 1) * 512], ws[:, c * 128:(c + 1) * 128],
                   xln_own[:, c * NT + th * 512: c * NT + (th + 1) * 512],
                   start=(c == 0), stop=(c == NDC - 1), tag=f"qp{pair}")
        acopy(qa, ps_q, tag=f"qa{pair}")
        for th in range(2):
            mm(ps_q[:, th * 512:(th + 1) * 512], rt_sb,
               qa[:, th * 512:(th + 1) * 512], start=True, stop=True,
               tag=f"qr{pair}_{th}")
        kv = (2 * pair) // 4
        beta = (kv % 2) * 64
        sE, sO = qslot[0], qslot[1]
        for th in range(2):
            sl = slice(th * 512, (th + 1) * 512)
            tt(t3h, qa[:, sl], crepq[:, sl], ALU.mult, tag=f"q3{pair}_{th}")
            tt(t4h, ps_q[:, sl], srepq[:, sl], ALU.mult, tag=f"q4{pair}_{th}")
            tt(sE[beta:beta + 64, sl], t3h[0:64, :], t4h[0:64, :], ALU.add,
               tag=f"qE{pair}_{th}")
            tt(sO[beta:beta + 64, sl], t3h[64:128, :], t4h[64:128, :], ALU.add,
               tag=f"qO{pair}_{th}")
        kchunk = kv // 2
        if DEBUG and pair == 0:
            dma("sp", d_dq[:, 0:NT], qslot[0], tag="dbgq0")
            dma("sp", d_dq[:, NT:2 * NT], qslot[1], tag="dbgq1")

        for qb in range(2):
            E = 16 * (qb + 1)
            ng = E // 2
            ctr = [0]

            def plan_scores(hh, g):
                h = 2 * pair + hh
                S = sE if hh == 0 else sO
                psc = ps_sc[ctr[0] % 3]
                Pl = pslot[ctr[0] % 4]
                for j in range(2):
                    kt = 2 * g + j
                    mm(psc[:, j * 512:(j + 1) * 512],
                       k_sb[beta:beta + 64,
                            kchunk * T + kt * 128: kchunk * T + (kt + 1) * 128],
                       S[beta:beta + 64, qb * 512:(qb + 1) * 512],
                       start=True, stop=True, tag=f"sc{h}_{qb}_{g}")
                act(Pl, psc, AF.Exp, bias=CN_EXPB, tag=f"ex{h}_{qb}_{g}")
                v0 = 2 * g - 16 * qb
                if v0 >= 0:
                    tt(Pl, Pl, msk_sb[:, v0 * 512:(v0 + 2) * 512], ALU.mult,
                       tag=f"mk{h}_{qb}_{g}")
                ctr[0] += 1
                return Pl

            def plan_av(hh, g, Pl):
                h = 2 * pair + hh
                kvh = h // 4
                for j in range(2):
                    kt = 2 * g + j
                    mm(ps_ctx[hh][0:65, :],
                       v_sb[:, kt * 520 + 65 * kvh: kt * 520 + 65 * kvh + 65],
                       Pl[:, j * 512:(j + 1) * 512],
                       start=(kt == 0), stop=(kt == E - 1),
                       tag=f"av{h}_{qb}_{g}")

            tasks = [(hh, g) for g in range(ng) for hh in (0, 1)]
            pend = []
            for t in tasks:
                Pl = plan_scores(*t)
                pend.append((t, Pl))
                if len(pend) > 2:
                    (t2, P2) = pend.pop(0)
                    plan_av(t2[0], t2[1], P2)
            for (t2, P2) in pend:
                plan_av(t2[0], t2[1], P2)

            for hh in range(2):
                h = 2 * pair + hh
                vrecip(lrow[64:65, :], ps_ctx[hh][64:65, :], tag=f"li{h}_{qb}")
                xps = ps_sc[(ctr[0] + 1) % 3]
                mm(xps[0:64, 0:512], ones_f[64:65, 0:64], lrow[64:65, :],
                   start=True, stop=True, tag=f"lb{h}_{qb}")
                acopy(linv_bc[0:64, :], xps[0:64, 0:512], tag=f"lv{h}_{qb}")
                cdst = ctx_sb[(h % 2) * 64:(h % 2) * 64 + 64,
                              (h // 2) * NT + qb * 512:(h // 2) * NT + (qb + 1) * 512]
                tt(cdst, ps_ctx[hh][0:64, :], linv_bc[0:64, :], ALU.mult,
                   tag=f"cs{h}_{qb}")

    if DEBUG:
        dma("sp", d_dxl, xln_own, tag="dbgxl")
        dma("sp", d_dk, k_sb, tag="dbgk")
        dma("sp", d_dv, v_sb, tag="dbgv")
        dma("sp", d_dctx, ctx_sb, tag="dbgctx")
    st2.close()
    p.barrier()

    dma("sp", resid.rearrange("p (c t) -> p c t", c=NDC),
        d_xot.rearrange("(c p) t -> p c t", p=128), tag="xot")
    if DEBUG:
        dma("sp", d_dx0, resid, tag="dbgx0")
    st3 = contextlib.ExitStack()
    ps_o = [st3.enter_context(nc.psum_tensor(f"ps_o{i}", [128, 512], F32)).ap()
            for i in range(2)]
    for oc in range(16):
        ws = woslab[oc % 2]
        dmac("gp", ws.rearrange("p (c m) -> p c m", c=NDC), d_wo[oc], tag=f"wo{oc}")
        lastmm = {}
        for th in range(2):
            po = ps_o[th]
            for cc in range(NDC):
                r = mm(po, ws[:, cc * 128:(cc + 1) * 128],
                       ctx_sb[:, cc * NT + th * 512: cc * NT + (th + 1) * 512],
                       start=(cc == 0), stop=(cc == NDC - 1), tag=f"wm{oc}_{th}")
            lastmm[th] = r
        for th in range(2):
            rdst = resid[:, oc * NT + th * 512: oc * NT + (th + 1) * 512]
            tt(rdst, ps_o[th], rdst, ALU.add, tag=f"ra{oc}_{th}",
               deps=(lastmm[1 - th],))
    if DEBUG:
        dma("sp", d_dx1, resid, tag="dbgx1")
    st3.close()
    p.barrier()

    x1b = at("x1b", [128, NDC * NT], BF16, R3)
    st4 = contextlib.ExitStack()
    ps_s4 = st4.enter_context(nc.psum_tensor("ps_s4", [1, 1024], F32)).ap()
    ps_b4 = st4.enter_context(nc.psum_tensor("ps_b4", [128, 1024], F32)).ap()
    ps_lg = st4.enter_context(nc.psum_tensor("ps_lg", [8, 1024], F32)).ap()
    ps_t4 = [st4.enter_context(nc.psum_tensor(f"ps_t4{i}", [128, 128], F32)).ap()
             for i in range(2)]
    rw_sb = at("rw_sb", [128, NDC * 8], BF16, R1 + 33 * KB - 512)

    for c in range(NDC):
        acopy(x1b[:, c * NT:(c + 1) * NT], resid[:, c * NT:(c + 1) * NT],
              tag=f"c1b{c}")
    for th in range(2):
        for c in range(NDC):
            xc = x1b[:, c * NT + th * 512: c * NT + (th + 1) * 512]
            tt(sq1, xc, xc, ALU.mult, tag=f"q2{th}_{c}")
            mm(ps_s4[0:1, 0:512], ones_bf, xc,
               start=(c == 0), stop=(c == NDC - 1), tag=f"zx{th}")
            mm(ps_s4[0:1, 512:1024], ones_bf, sq1,
               start=(c == 0), stop=(c == NDC - 1), tag=f"zq{th}")
        ts(r2m, ps_s4[0:1, 0:512], R_INV, ALU.mult, tag="2mean")
        ts(r2q, ps_s4[0:1, 512:1024], R_INV, ALU.mult, tag="2msq")
        tt(r2a, r2m, r2m, ALU.mult, tag="2m2")
        tt(r2a, r2q, r2a, ALU.subtract, tag="2var")
        act(r2a, r2a, AF.Sqrt, bias=R_EPS, tag="2std")
        vrecip(r2a, r2a, tag="2rstd")
        stt(r2b, r2m, R_NEG1, r2a, ALU.mult, ALU.mult, tag="2brow")
        mm(ps_b4[:, 0:512], ones_f[0:1, :], r2a, start=True, stop=True, tag="2ba")
        mm(ps_b4[:, 512:1024], ones_f[0:1, :], r2b, start=True, stop=True, tag="2bb")
        acopy(ab2_bc, ps_b4, tag="2ab")
        for c in range(NDC):
            xc = x1b[:, c * NT + th * 512: c * NT + (th + 1) * 512]
            tt(xc, xc, ab2_bc[:, 0:512], ALU.mult, tag=f"2lm{th}_{c}")
        for c in range(NDC):
            xc = x1b[:, c * NT + th * 512: c * NT + (th + 1) * 512]
            tt(xc, xc, ab2_bc[:, 512:1024], ALU.add, tag=f"2la{th}_{c}")
    xln1 = x1b

    dmac("gp", rw_sb.rearrange("p (c m) -> p c m", c=NDC), d_rw, tag="rw")
    for th in range(2):
        for c in range(NDC):
            mm(ps_lg[0:8, th * 512:(th + 1) * 512], rw_sb[:, c * 8:(c + 1) * 8],
               xln1[:, c * NT + th * 512: c * NT + (th + 1) * 512],
               start=(c == 0), stop=(c == NDC - 1), tag=f"lg{th}")
    acopy(lg_sb, ps_lg, tag="lgev")
    vmemset(onesrow_bf, 1.0)
    vmemset(onesrow5, 1.0)
    for tt_i in range(8):
        trr = tr(ps_t4[tt_i % 2][:, 0:8], lg_sb[0:8, tt_i * 128:(tt_i + 1) * 128],
                 tag=f"lt{tt_i}")
        acopy(ltile, ps_t4[tt_i % 2][:, 0:8], tag=f"lte{tt_i}")
        vredmax(gm, ltile, tag=f"gmax{tt_i}")
        act(gstore[:, tt_i * 9 + 8: tt_i * 9 + 9], gm, AF.Sigmoid, tag=f"gsc{tt_i}")
        ts(geq, ltile, gm, ALU.is_equal, tag=f"geq{tt_i}")
        vmemset(grun, 0.0, tag=f"gr0{tt_i}")
        for e8 in range(8):
            ts(gnr, grun, CN_NEG1, ALU.mult, CN_ONE, ALU.add, tag=f"gn{tt_i}_{e8}")
            tt(gstore[:, tt_i * 9 + e8: tt_i * 9 + e8 + 1], geq[:, e8:e8 + 1],
               gnr, ALU.mult, tag=f"gs{tt_i}_{e8}")
            tt(grun, grun, gstore[:, tt_i * 9 + e8: tt_i * 9 + e8 + 1], ALU.add,
               tag=f"ga{tt_i}_{e8}")

    def build_row(e8, dst_row, ps_pair, gsrc, gtm, tagp):
        for tt_i in range(8):
            pst = ps_pair[tt_i % 2]
            tr(pst[0:9, 0:128], gsrc[:, tt_i * 9: tt_i * 9 + 9],
               tag=f"{tagp}tr{tt_i}")
            acopy(gtm[0:9, :], pst[0:9, 0:128], tag=f"{tagp}te{tt_i}")
            dma("gp", dst_row[0:1, tt_i * 128:(tt_i + 1) * 128],
                gtm[e8:e8 + 1, :], tag=f"{tagp}rd{tt_i}")



    xs_sb = at("xs_sb", [128, NDC * NT], BF16, R2)
    build_row(8, rowtmp, ps_t4, gstore, gtmp, "sc")
    for th in range(2):
        mm(ps_b4[:, 0:512], onesrow_bf,
           rowtmp[0:1, th * 512:(th + 1) * 512], start=True, stop=True,
           tag=f"sb{th}")
        acopy(s_bc, ps_b4[:, 0:512], tag=f"sbe{th}")
        for c in range(NDC):
            tt(xs_sb[:, c * NT + th * 512: c * NT + (th + 1) * 512],
               xln1[:, c * NT + th * 512: c * NT + (th + 1) * 512], s_bc,
               ALU.mult, tag=f"xs{th}_{c}")
    if DEBUG:
        dma("sp", d_dlg, lg_sb, tag="dbglg")
        dma("sp", d_dg, gstore, tag="dbgg")
        dma("sp", d_dxs, xs_sb, tag="dbgxs")
    st4.close()
    p.barrier()

    st5 = contextlib.ExitStack()
    ring = [st5.enter_context(nc.psum_tensor(f"ring{i}", [128, 512], F32)).ap()
            for i in range(8)]
    rix = [0]

    def nring():
        r = ring[rix[0] % 8]
        rix[0] += 1
        return r

    def expert(src, gsl, usl, dsl, masked, tagp):
        for fc in range(NDC):
            gs = gslab[fc % 2]
            us = uslab[fc % 2]
            dmac("gp", gs.rearrange("p (c m) -> p c m", c=NDC), gsl[fc],
                 tag=f"{tagp}g{fc}")
            dmac("gp", us.rearrange("p (c m) -> p c m", c=NDC), usl[fc],
                 tag=f"{tagp}u{fc}")
            for th in range(2):
                rg = nring()
                for c in range(NDC):
                    mm(rg, gs[:, c * 128:(c + 1) * 128],
                       src[:, c * NT + th * 512: c * NT + (th + 1) * 512],
                       start=(c == 0), stop=(c == NDC - 1), tag=f"{tagp}G{fc}")
                act(s1s[th], rg, AF.Silu, tag=f"{tagp}si{fc}_{th}")
                ru = nring()
                for c in range(NDC):
                    mm(ru, us[:, c * 128:(c + 1) * 128],
                       src[:, c * NT + th * 512: c * NT + (th + 1) * 512],
                       start=(c == 0), stop=(c == NDC - 1), tag=f"{tagp}U{fc}")
                hdst = h_sb[:, fc * NT + th * 512: fc * NT + (th + 1) * 512]
                if masked:
                    tt(ums[th], ru, mskbc[:, th * 512:(th + 1) * 512], ALU.mult,
                       tag=f"{tagp}um{fc}_{th}")
                    tt(hdst, s1s[th], ums[th], ALU.mult, tag=f"{tagp}h{fc}_{th}")
                else:
                    tt(hdst, s1s[th], ru, ALU.mult, tag=f"{tagp}h{fc}_{th}")
        for oc in range(NDC):
            dsb = dslab[oc % 2]
            dmac("gp", dsb.rearrange("p (c m) -> p c m", c=NDC), dsl[oc],
                 tag=f"{tagp}d{oc}")
            for th in range(2):
                rd = nring()
                for c in range(NDC):
                    mm(rd, dsb[:, c * 128:(c + 1) * 128],
                       h_sb[:, c * NT + th * 512: c * NT + (th + 1) * 512],
                       start=(c == 0), stop=(c == NDC - 1), tag=f"{tagp}D{oc}")
                rdst = resid[:, oc * NT + th * 512: oc * NT + (th + 1) * 512]
                tt(rdst, rd, rdst, ALU.add, tag=f"{tagp}r{oc}_{th}")

    ps_t5 = [ring[6], ring[7]]
    for e8 in range(8):
        build_row(e8, rowtmp5, ps_t5, gstore, gtmp5, f"m{e8}")
        for th in range(2):
            rb = nring()
            mm(rb[:, 0:512], onesrow5,
               rowtmp5[0:1, th * 512:(th + 1) * 512], start=True, stop=True,
               tag=f"mb{e8}_{th}")
            acopy(mskbc[:, th * 512:(th + 1) * 512], rb[:, 0:512],
                  tag=f"mbe{e8}_{th}")
        expert(xs_sb, d_wg[e8], d_wu[e8], d_wd[e8], True, f"e{e8}")
    expert(xln1, d_wsg, d_wsu, d_wsd, False, "sh")
    st5.close()
    p.barrier()

    st6 = contextlib.ExitStack()
    ps_t6 = [st6.enter_context(nc.psum_tensor(f"ps_t6{i}", [128, 2048], F32)).ap()
             for i in range(2)]
    for tt_i in range(8):
        pst = ps_t6[tt_i % 2]
        for c in range(NDC):
            tr(pst[:, c * 128:(c + 1) * 128],
               resid[:, c * NT + tt_i * 128: c * NT + (tt_i + 1) * 128],
               tag=f"ot{tt_i}_{c}")
        og = ostg[tt_i % 2]
        acopy(og, pst, tag=f"oe{tt_i}")
        dma("sp", d_y[tt_i * 128:(tt_i + 1) * 128, :], og, tag=f"os{tt_i}")
    st6.close()
    p.final_dma_fence()

    p.finalize()

    with nc.Block() as blk:
        @blk.sync
        def _(e):
            p.run_stream("sp", e)

        @blk.gpsimd
        def _(e):
            p.run_stream("gp", e)

        @blk.tensor
        def _(e):
            p.run_stream("pe", e)

        @blk.scalar
        def _(e):
            p.run_stream("act", e)

        @blk.vector
        def _(e):
            p.run_stream("dve", e)

    return nc



_PROG = None


def _get_prog():
    global _PROG
    if _PROG is None:
        _PROG = build_program()
    return _PROG


def _rope_tables():
    theta = (1.0 / (np.float32(10000.0) **
                    (np.arange(0, 64, 2, dtype=np.float32) / np.float32(64.0))))
    pos = np.arange(T, dtype=np.float32)
    fr = np.outer(pos, theta).astype(np.float32)
    return np.cos(fr), np.sin(fr)


def _slab4(w):
    return np.ascontiguousarray(
        w.reshape(16, 128, 16, 128).transpose(2, 1, 0, 3))


def kernel(L0_attn_x, L0_attn_Wq, L0_attn_Wk, L0_attn_Wv, L0_attn_Wo,
           L0_ffn_router_w, L0_ffn_gate_w, L0_ffn_up_w, L0_ffn_down_w,
           L0_ffn_shared_gate_w, L0_ffn_shared_up_w, L0_ffn_shared_down_w):
    x = np.asarray(L0_attn_x, dtype=np.float32)
    nc = _get_prog()

    cosf, sinf = _rope_tables()
    crepk = np.tile(cosf.T, (4, 1)).astype(BFNP)
    srepk = np.tile(sinf.T, (4, 1)).astype(BFNP)

    ident = np.eye(128, dtype=np.float32)
    R = np.zeros((128, 128), dtype=np.float32)
    for blk0 in (0, 64):
        for r in range(32):
            R[blk0 + r, blk0 + 32 + r] = -1.0
            R[blk0 + 32 + r, blk0 + r] = 1.0
    rotT = np.ascontiguousarray(R.T).astype(BFNP)
    consts = np.tile(np.array([[-12.0, 1.0 / 2048.0, 1e-5, 1.0, 0.0, -1.0,
                                0.0, 0.0]], np.float32), (128, 1))

    wq_l = _slab4(np.asarray(L0_attn_Wq, np.float32))
    wo_l = _slab4(np.asarray(L0_attn_Wo, np.float32))
    wk_l = np.ascontiguousarray(
        np.asarray(L0_attn_Wk, np.float32).reshape(16, 128, 512).transpose(1, 0, 2))
    wv_l = np.ascontiguousarray(
        np.asarray(L0_attn_Wv, np.float32).reshape(16, 128, 512).transpose(1, 0, 2))
    rw_l = np.ascontiguousarray(
        np.asarray(L0_ffn_router_w, np.float32).reshape(16, 128, 8).transpose(1, 0, 2))
    wg_l = np.ascontiguousarray(np.asarray(L0_ffn_gate_w, np.float32)
                                .reshape(8, 16, 128, 16, 128).transpose(0, 3, 2, 1, 4))
    wu_l = np.ascontiguousarray(np.asarray(L0_ffn_up_w, np.float32)
                                .reshape(8, 16, 128, 16, 128).transpose(0, 3, 2, 1, 4))
    wd_l = np.ascontiguousarray(np.asarray(L0_ffn_down_w, np.float32)
                                .reshape(8, 16, 128, 16, 128).transpose(0, 3, 2, 1, 4))
    wsg_l = _slab4(np.asarray(L0_ffn_shared_gate_w, np.float32))
    wsu_l = _slab4(np.asarray(L0_ffn_shared_up_w, np.float32))
    wsd_l = _slab4(np.asarray(L0_ffn_shared_down_w, np.float32))

    in_maps = []
    for c in range(8):
        b, jj = c // 4, c % 4
        xbt = np.ascontiguousarray(x[b].T)
        xot = np.ascontiguousarray(x[b, jj::4, :].T)
        own = np.arange(NT) * 4 + jj
        crepq = np.tile(cosf[own].T, (4, 1)).astype(np.float32) / 8.0
        srepq = np.tile(sinf[own].T, (4, 1)).astype(np.float32) / 8.0
        sarr = np.arange(128)[:, None]
        qarr = np.arange(512)[None, :]
        cmask = np.zeros((128, 16 * 512), dtype=np.float32)
        for v in range(16):
            cmask[:, v * 512:(v + 1) * 512] = (
                128 * v + sarr <= 4 * qarr + jj).astype(np.float32)
        in_maps.append({
            "xbt": xbt, "xot": xot,
            "wq_l": wq_l, "wk_l": wk_l, "wv_l": wv_l, "wo_l": wo_l,
            "rw_l": rw_l, "wg_l": wg_l, "wu_l": wu_l, "wd_l": wd_l,
            "wsg_l": wsg_l, "wsu_l": wsu_l, "wsd_l": wsd_l,
            "identf": ident, "rotT": rotT,
            "crepk": crepk, "srepk": srepk,
            "crepq": crepq.astype(BFNP), "srepq": srepq.astype(BFNP),
            "cmask": cmask.astype(BFNP), "consts": consts,
        })

    res = run_bass_kernel_spmd(nc, in_maps, core_ids=list(range(8)))
    out = np.empty((2, T, D), dtype=np.float32)
    for b in range(2):
        shards = [res.results[4 * b + jj]["y"] for jj in range(4)]
        out[b] = np.stack(shards, axis=1).reshape(T, D)
    return out


# version 10 (final)
# speedup vs baseline: 1.2221x; 1.2221x over previous
import contextlib
import numpy as np
import ml_dtypes
import concourse.bass as bass
import concourse.mybir as mybir
from concourse.bass_utils import run_bass_kernel_spmd

F32 = mybir.dt.float32
BF16 = mybir.dt.bfloat16
AF = mybir.ActivationFunctionType
ALU = mybir.AluOpType
AX = mybir.AxisListType
BFNP = ml_dtypes.bfloat16

D = 2048
T = 4096
NT = 1024
NDC = 16
NBLK = 16
BT = 256
KB = 1024
DEBUG = False



class Plan:

    ENGINES = ("pe", "act", "dve", "sp", "gp")

    NLANES = 8

    def __init__(self, nc):
        self.nc = nc
        self.sems = {e: nc.alloc_semaphore("sem_" + e) for e in ("pe", "act", "dve")}
        self.recs = {e: [] for e in self.ENGINES}
        self.last_writer = {}
        self.readers = {}
        self.lane = {q: [nc.alloc_semaphore(f"ln_{q}{i}") for i in range(self.NLANES)]
                     for q in ("sp", "gp")}
        self.lane_n = {q: [0] * self.NLANES for q in ("sp", "gp")}
        self.lane_last = {q: [None] * self.NLANES for q in ("sp", "gp")}
        self.lane_i = {q: 0 for q in ("sp", "gp")}

    @staticmethod
    def _name(ap):
        t = getattr(ap, "tensor", None)
        n = getattr(t, "name", None)
        return n if n is not None else str(id(ap))

    def op(self, eng, fn, ins=(), outs=(), tag="", deps=(), dma=False):
        rec = {"eng": eng, "fn": fn, "inc": False, "val": None,
               "waits": [], "tag": tag, "semh": None}
        if dma:
            L = self.lane_i[eng] % self.NLANES
            self.lane_i[eng] += 1
            prev = self.lane_last[eng][L]
            if prev is not None:
                self._need(rec, prev, force=True)
            self.lane_n[eng][L] += 1
            rec["inc"] = True
            rec["semh"] = self.lane[eng][L]
            rec["val"] = 16 * self.lane_n[eng][L]
            self.lane_last[eng][L] = rec
        for w in deps:
            self._need(rec, w, force=True)
        for a in ins:
            self._need(rec, self.last_writer.get(self._name(a)))
        for a in outs:
            n = self._name(a)
            self._need(rec, self.last_writer.get(n))
            seen = set()
            for r in reversed(self.readers.get(n, ())):
                if r["eng"] not in seen:
                    seen.add(r["eng"])
                    self._need(rec, r)
        for a in outs:
            n = self._name(a)
            self.last_writer[n] = rec
            self.readers[n] = []
        for a in ins:
            self.readers.setdefault(self._name(a), []).append(rec)
        self.recs[eng].append(rec)
        return rec

    def _need(self, rec, w, force=False):
        if w is None or w is rec:
            return
        if force and w["eng"] == rec["eng"] and rec["eng"] in ("pe", "act", "dve"):
            w["inc"] = True
            for (x, _) in rec["waits"]:
                if x is w:
                    return
            rec["waits"].append((w, None))
            return
        if not force and w["eng"] == rec["eng"] and rec["eng"] in ("pe", "sp", "gp"):
            return
        w["inc"] = True
        for (x, _) in rec["waits"]:
            if x is w:
                return
        rec["waits"].append((w, None))

    def barrier(self):
        if getattr(self, "flush_fns", None):
            if not hasattr(self, "flush_prev"):
                self.flush_prev = {}
            for eng, fn, out in self.flush_fns:
                dep = self.flush_prev.get(eng)
                r = self.op(eng, fn, ins=(), outs=(out,), tag="flush",
                            deps=(dep,) if dep else ())
                self.flush_prev[eng] = r
        lasts = {}
        for e in ("pe", "act", "dve"):
            lasts[e] = [self.recs[e][-1]] if self.recs[e] else []
        for q in ("sp", "gp"):
            lasts[q] = [r for r in self.lane_last[q] if r is not None]
        for e in self.ENGINES:
            rec = {"eng": e, "fn": (lambda eng: eng.nop()), "inc": False,
                   "val": None, "waits": [], "tag": "barrier", "semh": None}
            for e2 in self.ENGINES:
                for w in lasts[e2]:
                    if e2 != e or e2 in ("sp", "gp"):
                        if w is not rec:
                            w["inc"] = True
                            rec["waits"].append((w, None))
            self.recs[e].append(rec)
        self.last_writer = {}
        self.readers = {}

    def final_dma_fence(self):
        for q in ("sp", "gp"):
            rec = {"eng": q, "fn": (lambda eng: eng.nop()), "inc": False,
                   "val": None, "waits": [], "tag": "fence", "semh": None}
            for w in self.lane_last[q]:
                if w is not None:
                    w["inc"] = True
                    rec["waits"].append((w, None))
            self.recs[q].append(rec)

    def finalize(self):
        for e in ("pe", "act", "dve"):
            tot = 0
            for rec in self.recs[e]:
                if rec["inc"]:
                    tot += 1
                    rec["val"] = tot
                    rec["semh"] = self.sems[e]
        for e in self.ENGINES:
            for rec in self.recs[e]:
                rec["rw"] = [(w["semh"], w["val"]) for (w, _) in rec["waits"]]

    def run_stream(self, ename, eng):
        waited = {}
        for rec in self.recs[ename]:
            for (sem, val) in sorted(rec["rw"], key=lambda t: t[1]):
                if waited.get(sem.name, -1) < val:
                    eng.wait_ge(sem, val)
                    waited[sem.name] = val
            inst = rec["fn"](eng)
            if rec["inc"]:
                inst.then_inc(rec["semh"], 16 if ename in ("sp", "gp") else 1)




def build_program():
    nc = bass.Bass("TRN2", target_bir_lowering=False, debug=False)

    DI = nc.dram_tensor
    d_xbt = DI("xbt", [D, T], F32, kind="ExternalInput").ap()
    d_xot = DI("xot", [D, NT], F32, kind="ExternalInput").ap()
    d_wq = DI("wq_l", [16, 128, 16, 128], F32, kind="ExternalInput").ap()
    d_wk = DI("wk_l", [128, 16, 512], F32, kind="ExternalInput").ap()
    d_wv = DI("wv_l", [128, 16, 512], F32, kind="ExternalInput").ap()
    d_wo = DI("wo_l", [16, 128, 16, 128], F32, kind="ExternalInput").ap()
    d_rw = DI("rw_l", [128, 16, 8], F32, kind="ExternalInput").ap()
    d_wg = DI("wg_l", [8, 16, 128, 16, 128], F32, kind="ExternalInput").ap()
    d_wu = DI("wu_l", [8, 16, 128, 16, 128], F32, kind="ExternalInput").ap()
    d_wd = DI("wd_l", [8, 16, 128, 16, 128], F32, kind="ExternalInput").ap()
    d_wsg = DI("wsg_l", [16, 128, 16, 128], F32, kind="ExternalInput").ap()
    d_wsu = DI("wsu_l", [16, 128, 16, 128], F32, kind="ExternalInput").ap()
    d_wsd = DI("wsd_l", [16, 128, 16, 128], F32, kind="ExternalInput").ap()
    d_id = DI("identf", [128, 128], F32, kind="ExternalInput").ap()
    d_rt = DI("rotT", [128, 128], BF16, kind="ExternalInput").ap()
    d_ck = DI("crepk", [128, T], BF16, kind="ExternalInput").ap()
    d_sk = DI("srepk", [128, T], BF16, kind="ExternalInput").ap()
    d_cq = DI("crepq", [128, NT], BF16, kind="ExternalInput").ap()
    d_sq = DI("srepq", [128, NT], BF16, kind="ExternalInput").ap()
    d_msk = DI("cmask", [128, 16 * 512], BF16, kind="ExternalInput").ap()
    d_cn = DI("consts", [128, 8], F32, kind="ExternalInput").ap()
    d_y = DI("y", [NT, D], F32, kind="ExternalOutput").ap()
    if DEBUG:
        d_dxl = DI("dbg_xlnown", [128, NDC * NT], BF16, kind="ExternalOutput").ap()
        d_dk = DI("dbg_k", [128, 4 * T], BF16, kind="ExternalOutput").ap()
        d_dv = DI("dbg_v", [128, 32 * 520], BF16, kind="ExternalOutput").ap()
        d_dq = DI("dbg_q", [128, 2 * NT], BF16, kind="ExternalOutput").ap()
        d_dp = DI("dbg_p", [128, 1024], BF16, kind="ExternalOutput").ap()
        d_dctx = DI("dbg_ctx", [128, NDC * NT], BF16, kind="ExternalOutput").ap()
        d_dx1 = DI("dbg_x1", [128, NDC * NT], F32, kind="ExternalOutput").ap()
        d_dx0 = DI("dbg_x0", [128, NDC * NT], F32, kind="ExternalOutput").ap()
        d_dlg = DI("dbg_lg", [8, NT], F32, kind="ExternalOutput").ap()
        d_dg = DI("dbg_g", [128, 72], F32, kind="ExternalOutput").ap()
        d_dxs = DI("dbg_xs", [128, NDC * NT], BF16, kind="ExternalOutput").ap()

    BASE = (nc.sbuf_base + 31) // 32 * 32
    R0 = BASE
    R1 = R0 + 32 * KB
    R2 = R1 + 33 * KB
    R3 = R2 + 32 * KB
    R5 = R3 + 32 * KB
    RES = R5 + 13 * KB
    END = RES + 64 * KB
    cap = nc.sbuf_base + nc.sbuf_bytes_remaining
    assert END <= cap, (END, cap)

    def at(name, shape, dtype, off):
        return nc.alloc_sbuf_tensor_at(name, shape, dtype, offset=off).ap()

    k_sb = at("k_sb", [128, 4 * T], BF16, R0)
    v_sb = at("v_sb", [128, 32 * 520], BF16, R1)
    xln_own = at("xln_own", [128, NDC * NT], BF16, R2)
    ctx_sb = at("ctx_sb", [128, NDC * NT], BF16, R3)
    resid = at("resid", [128, NDC * NT], F32, RES)

    _r5 = [R5]

    def r5(name, shape, dtype):
        nbytes = shape[1] * (4 if dtype == F32 else (2 if dtype == BF16 else 4))
        off = (_r5[0] + 31) // 32 * 32
        _r5[0] = off + nbytes
        assert _r5[0] <= R5 + 13 * KB, (name, _r5[0] - R5)
        return at(name, shape, dtype, off)

    consts = r5("consts_sb", [128, 8], F32)
    ones_bf = r5("ones_bf", [128, 1], BF16)
    ones_f = r5("ones_f", [128, 128], F32)
    ident = r5("ident_sb", [128, 128], F32)
    rt_sb = r5("rt_sb", [128, 128], BF16)
    gstore = r5("gstore", [128, 9 * 8], F32)
    gtmp = r5("gtmp", [16, 128], F32)
    lrow = r5("lrow", [128, 512], F32)
    cfj = r5("cfj", [128, 8], BF16)
    jnk_a = r5("jnk_a", [128, 8], F32)
    jnk_b = r5("jnk_b", [128, 8], F32)
    jnk_c = r5("jnk_c", [128, 8], F32)

    rows5 = [at(f"rows5_{i}", [1, 4 * 256], F32, RES + 48 * KB + 4 * KB * i)
             for i in range(2)]
    a_bc = [at(f"a_bc{i}", [128, 256], BF16, RES + 56 * KB + KB * i) for i in range(2)]
    b_bc = [at(f"b_bc{i}", [128, 256], BF16, RES + 58 * KB + KB * i) for i in range(2)]
    ka_t = at("ka_t", [128, 256], BF16, RES + 60 * KB)
    kt3 = at("kt3", [128, 256], BF16, RES + 60 * KB + 512)
    kt4 = at("kt4", [128, 256], BF16, RES + 60 * KB + 1536)

    wk_sb = at("wk_sb", [128, NDC * 512], BF16, RES)
    wv_sb = at("wv_sb", [128, NDC * 512], BF16, RES + 16 * KB)
    crepk = at("crepk_sb", [128, T], BF16, RES + 32 * KB)
    srepk = at("srepk_sb", [128, T], BF16, RES + 40 * KB)
    xblk = [at(f"xblk{i}", [128, NDC * BT], BF16, R3 + 8 * KB * i) for i in range(2)]
    xsq = [at(f"xsq{i}", [128, NDC * BT], BF16, R3 + 16 * KB + 8 * KB * i) for i in range(2)]

    crepq = at("crepq_sb", [128, NT], BF16, RES)
    srepq = at("srepq_sb", [128, NT], BF16, RES + 2 * KB)
    msk_sb = at("cmask_sb", [128, 16 * 512], BF16, RES + 4 * KB)
    pslot = [at(f"pslot{i}", [128, 1024], BF16, RES + 20 * KB + 2 * KB * i)
             for i in range(4)]
    qslot = [at(f"qslot{i}", [128, NT], BF16, RES + 28 * KB + 2 * KB * i)
             for i in range(2)]
    qa = at("qa", [128, NT], BF16, RES + 32 * KB)
    t3h = at("t3h", [128, 512], BF16, RES + 34 * KB)
    t4h = at("t4h", [128, 512], BF16, RES + 36 * KB)
    wqslab = [at(f"wqslab{i}", [128, NDC * 128], BF16, RES + 38 * KB + 4 * KB * i)
              for i in range(2)]
    linv_bc = at("linv_bc", [128, 512], BF16, RES + 46 * KB)
    woslab = [at(f"woslab{i}", [128, NDC * 128], BF16, R1 + 4 * KB * i)
              for i in range(2)]
    abo_bc = at("abo_bc", [128, 1024], BF16, RES + 48 * KB)
    sqo = at("sqo", [128, 512], BF16, RES + 52 * KB)
    ro_m = at("ro_m", [1, 512], F32, RES + 53 * KB)
    ro_q = at("ro_q", [1, 512], F32, RES + 55 * KB)
    ro_a = at("ro_a", [1, 512], F32, RES + 57 * KB)
    ro_b = at("ro_b", [1, 512], F32, RES + 59 * KB)

    ltile = at("ltile", [128, 8], F32, R1)
    sq1 = at("sq1", [128, 512], BF16, R1 + 64)
    r2a = at("r2a", [1, 512], F32, R1 + 1 * KB + 64)
    r2b = at("r2b", [1, 512], F32, R1 + 3 * KB + 64)
    r2m = at("r2m", [1, 512], F32, R1 + 5 * KB + 64)
    r2q = at("r2q", [1, 512], F32, R1 + 7 * KB + 64)
    ab2_bc = at("ab2_bc", [128, 1024], BF16, R1 + 9 * KB + 64)
    lg_sb = at("lg_sb", [8, NT], F32, R1 + 13 * KB + 64)
    gm = at("gm", [128, 1], F32, R1 + 17 * KB + 64)
    geq = at("geq", [128, 8], F32, R1 + 17 * KB + 128)
    grun = at("grun", [128, 1], F32, R1 + 17 * KB + 192)
    gnr = at("gnr", [128, 1], F32, R1 + 17 * KB + 256)
    rowtmp = at("rowtmp", [1, NT], BF16, R1 + 18 * KB)
    onesrow_bf = at("onesrow_bf", [1, 128], BF16, R1 + 20 * KB)
    s_bc = at("s_bc", [128, 512], BF16, R1 + 20 * KB + 512)
    gslab = [at(f"gslab{i}", [128, NDC * 128], BF16, R1 + 4 * KB * i) for i in range(2)]
    uslab = [at(f"uslab{i}", [128, NDC * 128], BF16, R1 + 8 * KB + 4 * KB * i)
             for i in range(2)]
    dslab = [at(f"dslab{i}", [128, NDC * 128], BF16, R1 + 16 * KB + 4 * KB * i)
             for i in range(2)]
    mskbc = at("mskbc", [128, NT], BF16, R1 + 24 * KB)
    s1s = [at(f"s1s{i}", [128, 512], BF16, R1 + 26 * KB + KB * i) for i in range(2)]
    ums = [at(f"ums{i}", [128, 512], BF16, R1 + 28 * KB + KB * i) for i in range(2)]
    rowtmp5 = at("rowtmp5", [1, NT], BF16, R1 + 30 * KB)
    onesrow5 = at("onesrow5", [1, 128], BF16, R1 + 32 * KB)
    gtmp5 = at("gtmp5", [16, 128], F32, R1 + 32 * KB + 256)

    h_sb = at("h_sb", [128, NDC * NT], BF16, R0)
    ostg = [at(f"ostg{i}", [128, D], F32, R0 + 8 * KB * i) for i in range(2)]

    p = Plan(nc)

    def dma(q, out, in_, tag="", slow=False):
        def f(e, out=out, in_=in_, slow=slow):
            if slow:
                return e.dma_start(out=out, in_=in_, allow_slow_non_contiguous=True)
            return e.dma_start(out=out, in_=in_)
        return p.op(q, f, ins=[in_], outs=[out], tag=tag, dma=True)

    dmac = dma

    def mm(out, lhsT, rhs, start, stop, tag=""):
        def f(e, out=out, lhsT=lhsT, rhs=rhs, start=start, stop=stop):
            return e.matmul(out, lhsT, rhs, start=start, stop=stop,
                            skip_group_check=True)
        return p.op("pe", f, ins=[lhsT, rhs], outs=[out], tag=tag)

    def tr(out, in_, tag=""):
        np_ = in_.shape[0]
        idv = ident[0:np_, 0:np_]

        def f(e, out=out, in_=in_, idv=idv):
            return e.transpose(out, in_, idv)
        return p.op("pe", f, ins=[in_, ident], outs=[out], tag=tag)

    def act(out, in_, func, bias=None, scale=None, tag=""):
        ins = [in_] + [x for x in (bias, scale)
                       if x is not None and not isinstance(x, float)]

        def f(e, out=out, in_=in_, func=func, bias=bias, scale=scale):
            kw = {}
            if bias is not None:
                kw["bias"] = bias
            if scale is not None:
                kw["scale"] = scale
            return e.activation(out, in_, func, **kw)
        return p.op("act", f, ins=ins, outs=[out], tag=tag)

    def acopy(out, in_, tag=""):
        def f(e, out=out, in_=in_):
            return e.copy(out, in_)
        return p.op("act", f, ins=[in_], outs=[out], tag=tag)

    def tt(out, in0, in1, op, tag="", deps=()):
        def f(e, out=out, in0=in0, in1=in1, op=op):
            return e.tensor_tensor(out, in0, in1, op)
        return p.op("dve", f, ins=[in0, in1], outs=[out], tag=tag, deps=deps)

    def ts(out, in0, s1, op, s2=None, op2=None, tag=""):
        ins = [in0] + [x for x in (s1, s2)
                       if x is not None and not isinstance(x, float)]

        def f(e, out=out, in0=in0, s1=s1, s2=s2, op=op, op2=op2):
            if op2 is None:
                return e.tensor_scalar(out, in0, s1, None, op)
            return e.tensor_scalar(out, in0, s1, s2, op, op2)
        return p.op("dve", f, ins=ins, outs=[out], tag=tag)

    def stt(out, in0, s, in1, op0, op1, tag=""):
        ins = [in0, in1] + ([s] if not isinstance(s, float) else [])

        def f(e, out=out, in0=in0, s=s, in1=in1, op0=op0, op1=op1):
            return e.scalar_tensor_tensor(out, in0, s, in1, op0, op1)
        return p.op("dve", f, ins=ins, outs=[out], tag=tag)

    def vcopy(out, in_, tag=""):
        def f(e, out=out, in_=in_):
            return e.tensor_copy(out, in_)
        return p.op("dve", f, ins=[in_], outs=[out], tag=tag)

    def vmemset(out, val, tag=""):
        def f(e, out=out, val=val):
            return e.memset(out, val)
        return p.op("dve", f, ins=[], outs=[out], tag=tag)

    def vrecip(out, in_, tag=""):
        def f(e, out=out, in_=in_):
            return e.reciprocal(out, in_)
        return p.op("dve", f, ins=[in_], outs=[out], tag=tag)

    def vredmax(out, in_, tag=""):
        def f(e, out=out, in_=in_):
            return e.reduce_max(out, in_, axis=AX.X)
        return p.op("dve", f, ins=[in_], outs=[out], tag=tag)

    def _fl_dve(e):
        return e.tensor_copy(jnk_b[:], jnk_a[:])

    def _fl_act(e):
        return e.copy(jnk_c[:], jnk_a[:])

    p.flush_fns = [("dve", _fl_dve, jnk_b), ("act", _fl_act, jnk_c)]

    CN_EXPB = consts[:, 0:1]
    R_INV = consts[0:1, 1:2]
    R_EPS = consts[0:1, 2:3]
    CN_ONE = consts[:, 3:4]
    CN_NEG1 = consts[:, 5:6]
    R_NEG1 = consts[0:1, 5:6]

    dma("sp", consts, d_cn)
    dma("sp", ident, d_id)
    dma("sp", rt_sb, d_rt)
    dma("sp", crepk, d_ck)
    dma("sp", srepk, d_sk)
    vmemset(jnk_a, 0.0)
    vmemset(cfj, 0.0)
    vmemset(ones_bf, 1.0)
    vmemset(ones_f, 1.0)
    vmemset(v_sb, 1.0)
    dmac("gp", wk_sb.rearrange("p (c m) -> p c m", c=NDC), d_wk, tag="wk")
    dmac("gp", wv_sb.rearrange("p (c m) -> p c m", c=NDC), d_wv, tag="wv")

    st1 = contextlib.ExitStack()
    ps_k = [st1.enter_context(nc.psum_tensor(f"ps_k{i}", [128, 256], F32)).ap()
            for i in range(2)]
    ps_v = [st1.enter_context(nc.psum_tensor(f"ps_v{i}", [128, 512], F32)).ap()
            for i in range(2)]
    ps_st = [st1.enter_context(nc.psum_tensor(f"ps_st{i}", [1, 512], F32)).ap()
             for i in range(2)]
    ps_bb = [st1.enter_context(nc.psum_tensor(f"ps_bb{i}", [128, 256], F32)).ap()
             for i in range(2)]

    def p1_load(bi):
        xb = xblk[bi % 2]
        t0 = bi * BT
        dmac("gp", xb.rearrange("p (c t) -> p c t", c=NDC),
             d_xbt[:, t0:t0 + BT].rearrange("(c p) t -> p c t", p=128),
             tag=f"xb{bi}")

    def p1_ln(bi):
        xb = xblk[bi % 2]
        xq = xsq[bi % 2]
        ab = a_bc[bi % 2]
        bb = b_bc[bi % 2]
        stp = ps_st[bi % 2]
        tt(xq, xb, xb, ALU.mult, tag=f"xsq{bi}")
        for c in range(NDC):
            mm(stp[0:1, 0:256], ones_bf, xb[:, c * BT:(c + 1) * BT],
               start=(c == 0), stop=(c == NDC - 1), tag=f"sx{bi}")
        for c in range(NDC):
            mm(stp[0:1, 256:512], ones_bf, xq[:, c * BT:(c + 1) * BT],
               start=False, stop=(c == NDC - 1), tag=f"sq{bi}")
        rw = rows5[bi % 2]
        r_mean, r_var, r_a, r_b = (rw[0:1, i * 256:(i + 1) * 256]
                                   for i in range(4))
        ts(r_mean, stp[0:1, 0:256], R_INV, ALU.mult, tag="mean")
        ts(r_var, stp[0:1, 256:512], R_INV, ALU.mult, tag="msq")
        tt(r_a, r_mean, r_mean, ALU.mult, tag="m2")
        tt(r_var, r_var, r_a, ALU.subtract, tag="var")
        act(r_a, r_var, AF.Sqrt, bias=R_EPS, tag="std")
        vrecip(r_a, r_a, tag="rstd")
        stt(r_b, r_mean, R_NEG1, r_a, ALU.mult, ALU.mult, tag="brow")
        mm(ps_bb[0], ones_f[0:1, :], r_a, start=True, stop=True, tag="bca")
        mm(ps_bb[1], ones_f[0:1, :], r_b, start=True, stop=True, tag="bcb")
        acopy(ab, ps_bb[0], tag="abc")
        acopy(bb, ps_bb[1], tag="bbc")

    def p1_apply(bi):
        xb = xblk[bi % 2]
        xq = xsq[bi % 2]
        ab = a_bc[bi % 2]
        bb = b_bc[bi % 2]
        for c in range(NDC):
            tt(xq[:, c * BT:(c + 1) * BT], xb[:, c * BT:(c + 1) * BT], ab,
               ALU.mult, tag=f"lm{bi}_{c}")
        for c in range(NDC):
            tt(xq[:, c * BT:(c + 1) * BT], xq[:, c * BT:(c + 1) * BT], bb,
               ALU.add, tag=f"la{bi}_{c}")

    def p1_kv(bi):
        t0 = bi * BT
        xln = xsq[bi % 2]
        for kc in range(4):
            pk = ps_k[kc % 2]
            for c in range(NDC):
                mm(pk, wk_sb[:, c * 512 + kc * 128: c * 512 + (kc + 1) * 128],
                   xln[:, c * BT:(c + 1) * BT],
                   start=(c == 0), stop=(c == NDC - 1), tag=f"kp{bi}_{kc}")
            acopy(ka_t, pk, tag=f"ka{bi}_{kc}")
            mm(pk, rt_sb, ka_t, start=True, stop=True, tag=f"kr{bi}_{kc}")
            tt(kt3, ka_t, crepk[:, t0:t0 + BT], ALU.mult, tag=f"k3{bi}_{kc}")
            tt(kt4, pk, srepk[:, t0:t0 + BT], ALU.mult, tag=f"k4{bi}_{kc}")
            tt(k_sb[:, kc * T + t0: kc * T + t0 + BT], kt3, kt4, ALU.add,
               tag=f"ks{bi}_{kc}")
        xln3 = xln.rearrange("p (c t) -> p c t", c=NDC)
        for sub in range(2):
            pv = ps_v[sub]
            for c in range(NDC):
                mm(pv, xln3[:, c, sub * 128:(sub + 1) * 128],
                   wv_sb[:, c * 512:(c + 1) * 512],
                   start=(c == 0), stop=(c == NDC - 1), tag=f"vp{bi}_{sub}")
            kt = bi * 2 + sub
            vdst = v_sb[:, kt * 520:(kt + 1) * 520].rearrange(
                "p (h e) -> p h e", h=8)[:, :, 0:64]
            acopy(vdst, pv.rearrange("p (h e) -> p h e", h=8), tag=f"ve{bi}_{sub}")

    p1_load(0)
    p1_load(1)
    p1_ln(0)
    p1_apply(0)
    for bi in range(NBLK):
        if bi + 2 < NBLK:
            p1_load(bi + 2)
        if bi + 1 < NBLK:
            p1_ln(bi + 1)
        p1_kv(bi)
        if bi + 1 < NBLK:
            p1_apply(bi + 1)

    st1.close()
    p.barrier()

    dmac("gp", xln_own.rearrange("p (c t) -> p c t", c=NDC),
         d_xot.rearrange("(c p) t -> p c t", p=128), tag="xob")
    dma("sp", crepq, d_cq)
    dma("sp", srepq, d_sq)
    dma("sp", msk_sb, d_msk)

    st2 = contextlib.ExitStack()
    ps_sc = [st2.enter_context(nc.psum_tensor(f"ps_sc{i}", [128, 1024], F32)).ap()
             for i in range(3)]
    ps_ctx = [st2.enter_context(nc.psum_tensor(f"ps_ctx{i}", [128, 512], F32)).ap()
              for i in range(2)]
    ps_q = ps_sc[1]

    for th in range(2):
        for c in range(NDC):
            xc = xln_own[:, c * NT + th * 512: c * NT + (th + 1) * 512]
            tt(sqo, xc, xc, ALU.mult, tag=f"oq{th}_{c}")
            mm(ps_q[0:1, 0:512], ones_bf, xc,
               start=(c == 0), stop=(c == NDC - 1), tag=f"osx{th}")
            mm(ps_q[0:1, 512:1024], ones_bf, sqo,
               start=(c == 0), stop=(c == NDC - 1), tag=f"osq{th}")
        ts(ro_m, ps_q[0:1, 0:512], R_INV, ALU.mult, tag="omean")
        ts(ro_q, ps_q[0:1, 512:1024], R_INV, ALU.mult, tag="omsq")
        tt(ro_a, ro_m, ro_m, ALU.mult, tag="om2")
        tt(ro_a, ro_q, ro_a, ALU.subtract, tag="ovar")
        act(ro_a, ro_a, AF.Sqrt, bias=R_EPS, tag="ostd")
        vrecip(ro_a, ro_a, tag="orstd")
        stt(ro_b, ro_m, R_NEG1, ro_a, ALU.mult, ALU.mult, tag="obrow")
        mm(ps_q[:, 0:512], ones_f[0:1, :], ro_a, start=True, stop=True, tag="oba")
        mm(ps_q[:, 512:1024], ones_f[0:1, :], ro_b, start=True, stop=True, tag="obb")
        acopy(abo_bc, ps_q, tag="oab")
        for c in range(NDC):
            xc = xln_own[:, c * NT + th * 512: c * NT + (th + 1) * 512]
            tt(xc, xc, abo_bc[:, 0:512], ALU.mult, tag=f"olm{th}_{c}")
        for c in range(NDC):
            xc = xln_own[:, c * NT + th * 512: c * NT + (th + 1) * 512]
            tt(xc, xc, abo_bc[:, 512:1024], ALU.add, tag=f"ola{th}_{c}")

    for pair in range(16):
        ws = wqslab[pair % 2]
        dmac("gp", ws.rearrange("p (c m) -> p c m", c=NDC), d_wq[pair],
             tag=f"wq{pair}")
        for th in range(2):
            for c in range(NDC):
                mm(ps_q[:, th * 512:(th + 1) * 512], ws[:, c * 128:(c + 1) * 128],
                   xln_own[:, c * NT + th * 512: c * NT + (th + 1) * 512],
                   start=(c == 0), stop=(c == NDC - 1), tag=f"qp{pair}")
        acopy(qa, ps_q, tag=f"qa{pair}")
        for th in range(2):
            mm(ps_q[:, th * 512:(th + 1) * 512], rt_sb,
               qa[:, th * 512:(th + 1) * 512], start=True, stop=True,
               tag=f"qr{pair}_{th}")
        kv = (2 * pair) // 4
        beta = (kv % 2) * 64
        sE, sO = qslot[0], qslot[1]
        for th in range(2):
            sl = slice(th * 512, (th + 1) * 512)
            tt(t3h, qa[:, sl], crepq[:, sl], ALU.mult, tag=f"q3{pair}_{th}")
            tt(t4h, ps_q[:, sl], srepq[:, sl], ALU.mult, tag=f"q4{pair}_{th}")
            tt(sE[beta:beta + 64, sl], t3h[0:64, :], t4h[0:64, :], ALU.add,
               tag=f"qE{pair}_{th}")
            tt(sO[beta:beta + 64, sl], t3h[64:128, :], t4h[64:128, :], ALU.add,
               tag=f"qO{pair}_{th}")
        kchunk = kv // 2
        if DEBUG and pair == 0:
            dma("sp", d_dq[:, 0:NT], qslot[0], tag="dbgq0")
            dma("sp", d_dq[:, NT:2 * NT], qslot[1], tag="dbgq1")

        for qb in range(2):
            E = 16 * (qb + 1)
            ng = E // 2
            ctr = [0]

            def plan_scores(hh, g):
                h = 2 * pair + hh
                S = sE if hh == 0 else sO
                psc = ps_sc[ctr[0] % 3]
                Pl = pslot[ctr[0] % 4]
                for j in range(2):
                    kt = 2 * g + j
                    mm(psc[:, j * 512:(j + 1) * 512],
                       k_sb[beta:beta + 64,
                            kchunk * T + kt * 128: kchunk * T + (kt + 1) * 128],
                       S[beta:beta + 64, qb * 512:(qb + 1) * 512],
                       start=True, stop=True, tag=f"sc{h}_{qb}_{g}")
                act(Pl, psc, AF.Exp, bias=CN_EXPB, tag=f"ex{h}_{qb}_{g}")
                v0 = 2 * g - 16 * qb
                if v0 >= 0:
                    tt(Pl, Pl, msk_sb[:, v0 * 512:(v0 + 2) * 512], ALU.mult,
                       tag=f"mk{h}_{qb}_{g}")
                ctr[0] += 1
                return Pl

            def plan_av(hh, g, Pl):
                h = 2 * pair + hh
                kvh = h // 4
                for j in range(2):
                    kt = 2 * g + j
                    mm(ps_ctx[hh][0:65, :],
                       v_sb[:, kt * 520 + 65 * kvh: kt * 520 + 65 * kvh + 65],
                       Pl[:, j * 512:(j + 1) * 512],
                       start=(kt == 0), stop=(kt == E - 1),
                       tag=f"av{h}_{qb}_{g}")

            tasks = [(hh, g) for g in range(ng) for hh in (0, 1)]
            pend = []
            for t in tasks:
                Pl = plan_scores(*t)
                pend.append((t, Pl))
                if len(pend) > 2:
                    (t2, P2) = pend.pop(0)
                    plan_av(t2[0], t2[1], P2)
            for (t2, P2) in pend:
                plan_av(t2[0], t2[1], P2)

            for hh in range(2):
                h = 2 * pair + hh
                vrecip(lrow[64:65, :], ps_ctx[hh][64:65, :], tag=f"li{h}_{qb}")
                xps = ps_sc[(ctr[0] + 1) % 3]
                mm(xps[0:64, 0:512], ones_f[64:65, 0:64], lrow[64:65, :],
                   start=True, stop=True, tag=f"lb{h}_{qb}")
                acopy(linv_bc[0:64, :], xps[0:64, 0:512], tag=f"lv{h}_{qb}")
                cdst = ctx_sb[(h % 2) * 64:(h % 2) * 64 + 64,
                              (h // 2) * NT + qb * 512:(h // 2) * NT + (qb + 1) * 512]
                tt(cdst, ps_ctx[hh][0:64, :], linv_bc[0:64, :], ALU.mult,
                   tag=f"cs{h}_{qb}")

    if DEBUG:
        dma("sp", d_dxl, xln_own, tag="dbgxl")
        dma("sp", d_dk, k_sb, tag="dbgk")
        dma("sp", d_dv, v_sb, tag="dbgv")
        dma("sp", d_dctx, ctx_sb, tag="dbgctx")
    st2.close()
    p.barrier()

    dma("sp", resid.rearrange("p (c t) -> p c t", c=NDC),
        d_xot.rearrange("(c p) t -> p c t", p=128), tag="xot")
    if DEBUG:
        dma("sp", d_dx0, resid, tag="dbgx0")
    st3 = contextlib.ExitStack()
    ps_o = [st3.enter_context(nc.psum_tensor(f"ps_o{i}", [128, 512], F32)).ap()
            for i in range(2)]
    for oc in range(16):
        ws = woslab[oc % 2]
        dmac("gp", ws.rearrange("p (c m) -> p c m", c=NDC), d_wo[oc], tag=f"wo{oc}")
        lastmm = {}
        for th in range(2):
            po = ps_o[th]
            for cc in range(NDC):
                r = mm(po, ws[:, cc * 128:(cc + 1) * 128],
                       ctx_sb[:, cc * NT + th * 512: cc * NT + (th + 1) * 512],
                       start=(cc == 0), stop=(cc == NDC - 1), tag=f"wm{oc}_{th}")
            lastmm[th] = r
        for th in range(2):
            rdst = resid[:, oc * NT + th * 512: oc * NT + (th + 1) * 512]
            tt(rdst, ps_o[th], rdst, ALU.add, tag=f"ra{oc}_{th}",
               deps=(lastmm[1 - th],))
    if DEBUG:
        dma("sp", d_dx1, resid, tag="dbgx1")
    st3.close()
    p.barrier()

    x1b = at("x1b", [128, NDC * NT], BF16, R3)
    st4 = contextlib.ExitStack()
    ps_s4 = st4.enter_context(nc.psum_tensor("ps_s4", [1, 1024], F32)).ap()
    ps_b4 = st4.enter_context(nc.psum_tensor("ps_b4", [128, 1024], F32)).ap()
    ps_lg = st4.enter_context(nc.psum_tensor("ps_lg", [8, 1024], F32)).ap()
    ps_t4 = [st4.enter_context(nc.psum_tensor(f"ps_t4{i}", [128, 128], F32)).ap()
             for i in range(2)]
    rw_sb = at("rw_sb", [128, NDC * 8], BF16, R1 + 33 * KB - 512)

    for c in range(NDC):
        acopy(x1b[:, c * NT:(c + 1) * NT], resid[:, c * NT:(c + 1) * NT],
              tag=f"c1b{c}")
    for th in range(2):
        for c in range(NDC):
            xc = x1b[:, c * NT + th * 512: c * NT + (th + 1) * 512]
            tt(sq1, xc, xc, ALU.mult, tag=f"q2{th}_{c}")
            mm(ps_s4[0:1, 0:512], ones_bf, xc,
               start=(c == 0), stop=(c == NDC - 1), tag=f"zx{th}")
            mm(ps_s4[0:1, 512:1024], ones_bf, sq1,
               start=(c == 0), stop=(c == NDC - 1), tag=f"zq{th}")
        ts(r2m, ps_s4[0:1, 0:512], R_INV, ALU.mult, tag="2mean")
        ts(r2q, ps_s4[0:1, 512:1024], R_INV, ALU.mult, tag="2msq")
        tt(r2a, r2m, r2m, ALU.mult, tag="2m2")
        tt(r2a, r2q, r2a, ALU.subtract, tag="2var")
        act(r2a, r2a, AF.Sqrt, bias=R_EPS, tag="2std")
        vrecip(r2a, r2a, tag="2rstd")
        stt(r2b, r2m, R_NEG1, r2a, ALU.mult, ALU.mult, tag="2brow")
        mm(ps_b4[:, 0:512], ones_f[0:1, :], r2a, start=True, stop=True, tag="2ba")
        mm(ps_b4[:, 512:1024], ones_f[0:1, :], r2b, start=True, stop=True, tag="2bb")
        acopy(ab2_bc, ps_b4, tag="2ab")
        for c in range(NDC):
            xc = x1b[:, c * NT + th * 512: c * NT + (th + 1) * 512]
            tt(xc, xc, ab2_bc[:, 0:512], ALU.mult, tag=f"2lm{th}_{c}")
        for c in range(NDC):
            xc = x1b[:, c * NT + th * 512: c * NT + (th + 1) * 512]
            tt(xc, xc, ab2_bc[:, 512:1024], ALU.add, tag=f"2la{th}_{c}")
    xln1 = x1b

    dmac("gp", rw_sb.rearrange("p (c m) -> p c m", c=NDC), d_rw, tag="rw")
    for th in range(2):
        for c in range(NDC):
            mm(ps_lg[0:8, th * 512:(th + 1) * 512], rw_sb[:, c * 8:(c + 1) * 8],
               xln1[:, c * NT + th * 512: c * NT + (th + 1) * 512],
               start=(c == 0), stop=(c == NDC - 1), tag=f"lg{th}")
    acopy(lg_sb, ps_lg, tag="lgev")
    vmemset(onesrow_bf, 1.0)
    vmemset(onesrow5, 1.0)
    for tt_i in range(8):
        trr = tr(ps_t4[tt_i % 2][:, 0:8], lg_sb[0:8, tt_i * 128:(tt_i + 1) * 128],
                 tag=f"lt{tt_i}")
        acopy(ltile, ps_t4[tt_i % 2][:, 0:8], tag=f"lte{tt_i}")
        vredmax(gm, ltile, tag=f"gmax{tt_i}")
        act(gstore[:, tt_i * 9 + 8: tt_i * 9 + 9], gm, AF.Sigmoid, tag=f"gsc{tt_i}")
        ts(geq, ltile, gm, ALU.is_equal, tag=f"geq{tt_i}")
        vmemset(grun, 0.0, tag=f"gr0{tt_i}")
        for e8 in range(8):
            ts(gnr, grun, CN_NEG1, ALU.mult, CN_ONE, ALU.add, tag=f"gn{tt_i}_{e8}")
            tt(gstore[:, tt_i * 9 + e8: tt_i * 9 + e8 + 1], geq[:, e8:e8 + 1],
               gnr, ALU.mult, tag=f"gs{tt_i}_{e8}")
            tt(grun, grun, gstore[:, tt_i * 9 + e8: tt_i * 9 + e8 + 1], ALU.add,
               tag=f"ga{tt_i}_{e8}")

    def build_row(e8, dst_row, ps_pair, gsrc, gtm, tagp):
        for tt_i in range(8):
            pst = ps_pair[tt_i % 2]
            tr(pst[0:9, 0:128], gsrc[:, tt_i * 9: tt_i * 9 + 9],
               tag=f"{tagp}tr{tt_i}")
            acopy(gtm[0:9, :], pst[0:9, 0:128], tag=f"{tagp}te{tt_i}")
            dma("gp", dst_row[0:1, tt_i * 128:(tt_i + 1) * 128],
                gtm[e8:e8 + 1, :], tag=f"{tagp}rd{tt_i}")



    xs_sb = at("xs_sb", [128, NDC * NT], BF16, R2)
    build_row(8, rowtmp, ps_t4, gstore, gtmp, "sc")
    for th in range(2):
        mm(ps_b4[:, 0:512], onesrow_bf,
           rowtmp[0:1, th * 512:(th + 1) * 512], start=True, stop=True,
           tag=f"sb{th}")
        acopy(s_bc, ps_b4[:, 0:512], tag=f"sbe{th}")
        for c in range(NDC):
            tt(xs_sb[:, c * NT + th * 512: c * NT + (th + 1) * 512],
               xln1[:, c * NT + th * 512: c * NT + (th + 1) * 512], s_bc,
               ALU.mult, tag=f"xs{th}_{c}")
    if DEBUG:
        dma("sp", d_dlg, lg_sb, tag="dbglg")
        dma("sp", d_dg, gstore, tag="dbgg")
        dma("sp", d_dxs, xs_sb, tag="dbgxs")
    st4.close()
    p.barrier()

    st5 = contextlib.ExitStack()
    ring = [st5.enter_context(nc.psum_tensor(f"ring{i}", [128, 512], F32)).ap()
            for i in range(8)]
    rix = [0]

    def nring():
        r = ring[rix[0] % 8]
        rix[0] += 1
        return r

    def expert(src, gsl, usl, dsl, masked, tagp):
        for fc in range(NDC):
            gs = gslab[fc % 2]
            us = uslab[fc % 2]
            dmac("gp", gs.rearrange("p (c m) -> p c m", c=NDC), gsl[fc],
                 tag=f"{tagp}g{fc}")
            dmac("gp", us.rearrange("p (c m) -> p c m", c=NDC), usl[fc],
                 tag=f"{tagp}u{fc}")
            for th in range(2):
                rg = nring()
                for c in range(NDC):
                    mm(rg, gs[:, c * 128:(c + 1) * 128],
                       src[:, c * NT + th * 512: c * NT + (th + 1) * 512],
                       start=(c == 0), stop=(c == NDC - 1), tag=f"{tagp}G{fc}")
                act(s1s[th], rg, AF.Silu, tag=f"{tagp}si{fc}_{th}")
                ru = nring()
                for c in range(NDC):
                    mm(ru, us[:, c * 128:(c + 1) * 128],
                       src[:, c * NT + th * 512: c * NT + (th + 1) * 512],
                       start=(c == 0), stop=(c == NDC - 1), tag=f"{tagp}U{fc}")
                hdst = h_sb[:, fc * NT + th * 512: fc * NT + (th + 1) * 512]
                if masked:
                    tt(ums[th], ru, mskbc[:, th * 512:(th + 1) * 512], ALU.mult,
                       tag=f"{tagp}um{fc}_{th}")
                    tt(hdst, s1s[th], ums[th], ALU.mult, tag=f"{tagp}h{fc}_{th}")
                else:
                    tt(hdst, s1s[th], ru, ALU.mult, tag=f"{tagp}h{fc}_{th}")
        for oc in range(NDC):
            dsb = dslab[oc % 2]
            dmac("gp", dsb.rearrange("p (c m) -> p c m", c=NDC), dsl[oc],
                 tag=f"{tagp}d{oc}")
            for th in range(2):
                rd = nring()
                for c in range(NDC):
                    mm(rd, dsb[:, c * 128:(c + 1) * 128],
                       h_sb[:, c * NT + th * 512: c * NT + (th + 1) * 512],
                       start=(c == 0), stop=(c == NDC - 1), tag=f"{tagp}D{oc}")
                rdst = resid[:, oc * NT + th * 512: oc * NT + (th + 1) * 512]
                tt(rdst, rd, rdst, ALU.add, tag=f"{tagp}r{oc}_{th}")

    ps_t5 = [ring[6], ring[7]]
    for e8 in range(8):
        build_row(e8, rowtmp5, ps_t5, gstore, gtmp5, f"m{e8}")
        for th in range(2):
            rb = nring()
            mm(rb[:, 0:512], onesrow5,
               rowtmp5[0:1, th * 512:(th + 1) * 512], start=True, stop=True,
               tag=f"mb{e8}_{th}")
            acopy(mskbc[:, th * 512:(th + 1) * 512], rb[:, 0:512],
                  tag=f"mbe{e8}_{th}")
        expert(xs_sb, d_wg[e8], d_wu[e8], d_wd[e8], True, f"e{e8}")
    expert(xln1, d_wsg, d_wsu, d_wsd, False, "sh")
    st5.close()
    p.barrier()

    st6 = contextlib.ExitStack()
    ps_t6 = [st6.enter_context(nc.psum_tensor(f"ps_t6{i}", [128, 2048], F32)).ap()
             for i in range(2)]
    for tt_i in range(8):
        pst = ps_t6[tt_i % 2]
        for c in range(NDC):
            tr(pst[:, c * 128:(c + 1) * 128],
               resid[:, c * NT + tt_i * 128: c * NT + (tt_i + 1) * 128],
               tag=f"ot{tt_i}_{c}")
        og = ostg[tt_i % 2]
        acopy(og, pst, tag=f"oe{tt_i}")
        dma("sp", d_y[tt_i * 128:(tt_i + 1) * 128, :], og, tag=f"os{tt_i}")
    st6.close()
    p.final_dma_fence()

    p.finalize()

    with nc.Block() as blk:
        @blk.sync
        def _(e):
            p.run_stream("sp", e)

        @blk.gpsimd
        def _(e):
            p.run_stream("gp", e)

        @blk.tensor
        def _(e):
            p.run_stream("pe", e)

        @blk.scalar
        def _(e):
            p.run_stream("act", e)

        @blk.vector
        def _(e):
            p.run_stream("dve", e)

    return nc



_PROG = None


def _get_prog():
    global _PROG
    if _PROG is None:
        _PROG = build_program()
    return _PROG


def _rope_tables():
    theta = (1.0 / (np.float32(10000.0) **
                    (np.arange(0, 64, 2, dtype=np.float32) / np.float32(64.0))))
    pos = np.arange(T, dtype=np.float32)
    fr = np.outer(pos, theta).astype(np.float32)
    return np.cos(fr), np.sin(fr)


def _slab4(w):
    return np.ascontiguousarray(
        w.reshape(16, 128, 16, 128).transpose(2, 1, 0, 3))


def kernel(L0_attn_x, L0_attn_Wq, L0_attn_Wk, L0_attn_Wv, L0_attn_Wo,
           L0_ffn_router_w, L0_ffn_gate_w, L0_ffn_up_w, L0_ffn_down_w,
           L0_ffn_shared_gate_w, L0_ffn_shared_up_w, L0_ffn_shared_down_w):
    x = np.asarray(L0_attn_x, dtype=np.float32)
    nc = _get_prog()

    cosf, sinf = _rope_tables()
    crepk = np.tile(cosf.T, (4, 1)).astype(BFNP)
    srepk = np.tile(sinf.T, (4, 1)).astype(BFNP)

    ident = np.eye(128, dtype=np.float32)
    R = np.zeros((128, 128), dtype=np.float32)
    for blk0 in (0, 64):
        for r in range(32):
            R[blk0 + r, blk0 + 32 + r] = -1.0
            R[blk0 + 32 + r, blk0 + r] = 1.0
    rotT = np.ascontiguousarray(R.T).astype(BFNP)
    consts = np.tile(np.array([[-12.0, 1.0 / 2048.0, 1e-5, 1.0, 0.0, -1.0,
                                0.0, 0.0]], np.float32), (128, 1))

    wq_l = _slab4(np.asarray(L0_attn_Wq, np.float32))
    wo_l = _slab4(np.asarray(L0_attn_Wo, np.float32))
    wk_l = np.ascontiguousarray(
        np.asarray(L0_attn_Wk, np.float32).reshape(16, 128, 512).transpose(1, 0, 2))
    wv_l = np.ascontiguousarray(
        np.asarray(L0_attn_Wv, np.float32).reshape(16, 128, 512).transpose(1, 0, 2))
    rw_l = np.ascontiguousarray(
        np.asarray(L0_ffn_router_w, np.float32).reshape(16, 128, 8).transpose(1, 0, 2))
    wg_l = np.ascontiguousarray(np.asarray(L0_ffn_gate_w, np.float32)
                                .reshape(8, 16, 128, 16, 128).transpose(0, 3, 2, 1, 4))
    wu_l = np.ascontiguousarray(np.asarray(L0_ffn_up_w, np.float32)
                                .reshape(8, 16, 128, 16, 128).transpose(0, 3, 2, 1, 4))
    wd_l = np.ascontiguousarray(np.asarray(L0_ffn_down_w, np.float32)
                                .reshape(8, 16, 128, 16, 128).transpose(0, 3, 2, 1, 4))
    wsg_l = _slab4(np.asarray(L0_ffn_shared_gate_w, np.float32))
    wsu_l = _slab4(np.asarray(L0_ffn_shared_up_w, np.float32))
    wsd_l = _slab4(np.asarray(L0_ffn_shared_down_w, np.float32))

    in_maps = []
    for c in range(8):
        b, jj = c // 4, c % 4
        xbt = np.ascontiguousarray(x[b].T)
        xot = np.ascontiguousarray(x[b, jj::4, :].T)
        own = np.arange(NT) * 4 + jj
        crepq = np.tile(cosf[own].T, (4, 1)).astype(np.float32) / 8.0
        srepq = np.tile(sinf[own].T, (4, 1)).astype(np.float32) / 8.0
        sarr = np.arange(128)[:, None]
        qarr = np.arange(512)[None, :]
        cmask = np.zeros((128, 16 * 512), dtype=np.float32)
        for v in range(16):
            cmask[:, v * 512:(v + 1) * 512] = (
                128 * v + sarr <= 4 * qarr + jj).astype(np.float32)
        in_maps.append({
            "xbt": xbt, "xot": xot,
            "wq_l": wq_l, "wk_l": wk_l, "wv_l": wv_l, "wo_l": wo_l,
            "rw_l": rw_l, "wg_l": wg_l, "wu_l": wu_l, "wd_l": wd_l,
            "wsg_l": wsg_l, "wsu_l": wsu_l, "wsd_l": wsd_l,
            "identf": ident, "rotT": rotT,
            "crepk": crepk, "srepk": srepk,
            "crepq": crepq.astype(BFNP), "srepq": srepq.astype(BFNP),
            "cmask": cmask.astype(BFNP), "consts": consts,
        })

    res = run_bass_kernel_spmd(nc, in_maps, core_ids=list(range(8)))
    out = np.empty((2, T, D), dtype=np.float32)
    for b in range(2):
        shards = [res.results[4 * b + jj]["y"] for jj in range(4)]
        out[b] = np.stack(shards, axis=1).reshape(T, D)
    return out
